# version 08
# baseline (speedup 1.0000x reference)
import math
import numpy as np
import concourse.bass as bass
import concourse.mybir as mybir
from concourse.bass_utils import run_bass_kernel_spmd

F32 = mybir.dt.float32
BF16 = mybir.dt.bfloat16
I16 = mybir.dt.int16
AF = mybir.ActivationFunctionType
ALU = mybir.AluOpType
AX = mybir.AxisListType

T = 4096
D = 1024
H = 16
DK = 64
PM = 128
CH = 128
NCH = T // CH
SBN = 4
NSB = NCH // SBN
KC = D // 128
S_XN = 64.0 ** -0.25
SQC = -0.0625
C0 = 0.5 * math.log(128.0)
LN_EPS = 1e-5
PHI_EPS = 1e-6
DEN_EPS = 1e-6

STAGE = "full"


class Buf:
    __slots__ = ("writer", "readers", "name")

    def __init__(self, name=""):
        self.writer = None
        self.readers = []
        self.name = name


class Prog:
    QUEUES = ("sync", "scalar", "gpsimd")

    def __init__(self, nc):
        self.nc = nc
        self.plan = {e: [] for e in ("sync", "tensor", "vector", "scalar", "gpsimd")}
        self.count = {e: 0 for e in ("tensor", "vector", "scalar", "gpsimd")}
        for q in self.QUEUES:
            self.count["dma_" + q] = 0
        self.known = {e: {} for e in ("sync", "tensor", "vector", "scalar", "gpsimd")}
        self.hist = {s: [] for s in self.count}
        self.n_groups = 0
        self.flush_fns = {}
        self.flush_hist = {e: [] for e in ("vector", "scalar", "gpsimd")}

    def flush(self, eng):
        ffn, fbuf = self.flush_fns[eng]
        cov = self.count[eng]
        ftok = self.emit(eng, [ffn], writes=[fbuf])
        self.flush_hist[eng].append((cov, ftok[1]))
        return ftok

    def _dep(self, eng, token, waits, strict=False):
        if token is None:
            return
        sem, val = token
        if sem == eng and not strict:
            return
        if self.known[eng].get(sem, 0) >= val:
            return
        waits[sem] = max(waits.get(sem, 0), val)

    def emit(self, eng, fns, reads=(), writes=(), dma=False, strict=True):
        assert not dma or eng in self.QUEUES
        waits = {}
        for b in reads:
            self._dep(eng, b.writer, waits, strict)
        for b in writes:
            self._dep(eng, b.writer, waits, strict)
            for r in b.readers:
                self._dep(eng, r, waits, strict)
        for sem, val in waits.items():
            kn = self.known[eng]
            kn[sem] = max(kn.get(sem, 0), val)
            snap = None
            for v, cl in reversed(self.hist[sem]):
                if v <= val:
                    snap = cl
                    break
            if snap:
                for s2, v2 in snap.items():
                    if kn.get(s2, 0) < v2:
                        kn[s2] = v2
        if eng == "tensor" and self.flush_fns:
            for s in list(waits):
                if s in ("vector", "scalar", "gpsimd"):
                    val = waits[s]
                    del waits[s]
                    tokv = None
                    for (cov, fv) in self.flush_hist[s]:
                        if cov >= val:
                            tokv = fv
                            break
                    if tokv is None:
                        ffn, fbuf = self.flush_fns[s]
                        cov = self.count[s]
                        ftok = self.emit(s, [ffn], writes=[fbuf])
                        tokv = ftok[1]
                        self.flush_hist[s].append((cov, tokv))
                    waits[s] = tokv
                    self.known[eng][s] = max(self.known[eng].get(s, 0), tokv)
        sem = ("dma_" + eng) if dma else eng
        self.count[sem] += 16 * len(fns) if dma else 1
        val = self.count[sem]
        self.plan[eng].append((dict(waits), list(fns), sem, dma))
        token = (sem, val)
        snap = dict(self.known[eng])
        snap[sem] = val
        self.hist[sem].append((val, snap))
        for b in writes:
            b.writer = token
            b.readers = []
        for b in reads:
            b.readers.append(token)
        self.n_groups += 1
        return token

    def finalize(self, final_tokens):
        nc = self.nc
        sem_handles = {}
        ctxs = []
        for s in self.count:
            cm = nc.semaphore("sem_" + s)
            sem_handles[s] = cm.__enter__()
            ctxs.append(cm)
        blk_cm = nc.Block()
        block = blk_cm.__enter__()
        try:
            def make_body(ename):
                entries = self.plan[ename]

                def body(eng):
                    for waits, fns, sem, dma in entries:
                        for ws in sorted(waits):
                            eng.wait_ge(sem_handles[ws], waits[ws])
                        for i, f in enumerate(fns):
                            inst = f(eng)
                            if dma:
                                inst.then_inc(sem_handles[sem], 16)
                            elif i == len(fns) - 1:
                                inst.then_inc(sem_handles[sem], 1)
                    if ename == "sync":
                        for s2, v2 in final_tokens:
                            eng.wait_ge(sem_handles[s2], v2)
                return body

            for name, deco in (("sync", block.sync), ("tensor", block.tensor),
                               ("vector", block.vector), ("scalar", block.scalar),
                               ("gpsimd", block.gpsimd)):
                deco(make_body(name))
        finally:
            blk_cm.__exit__(None, None, None)
            for cm in ctxs:
                cm.__exit__(None, None, None)


def build_core_program(verbose=False):
    nc = bass.Bass("TRN2", target_bir_lowering=False)
    p = Prog(nc)
    E = p.emit

    xh = nc.dram_tensor("x", [T, D], F32, kind="ExternalInput").ap()
    ch = nc.dram_tensor("c", [D], F32, kind="ExternalInput").ap()
    wmodh = nc.dram_tensor("wmod", [D, 6 * D], F32, kind="ExternalInput").ap()
    wh = {}
    for nm in ("wq", "wk", "wv", "wo", "wf"):
        wh[nm] = nc.dram_tensor(nm, [D, D], F32, kind="ExternalInput").ap()
    bh = {}
    for nm in ("bq", "bk", "bv", "bo", "bf"):
        bh[nm] = nc.dram_tensor(nm, [D], F32, kind="ExternalInput").ap()
    projh = nc.dram_tensor("proj", [PM, DK], F32, kind="ExternalInput").ap()
    outh = nc.dram_tensor("out", [T, D], F32, kind="ExternalOutput").ap()
    scr = nc.dram_tensor("scratch", [8, 128], F32, kind="Internal").ap()

    def sb(name, shape, dt):
        return nc.alloc_sbuf_tensor(name, list(shape), dt).ap()

    def psum(name, shape):
        return nc.alloc_psum_tensor(name, list(shape), F32).ap()

    WT = {w: [sb(f"{w}T{k}", (128, D), BF16) for k in range(KC)]
          for w in ("wq", "wk", "wv", "wo", "wf")}
    qf = [sb(f"qf{h}", (128, SBN * CH), BF16) for h in range(H)]
    kf = [sb(f"kf{h}", (128, SBN * CH), BF16) for h in range(H)]
    kftm = [sb(f"kftm{h}", (128, SBN * CH), BF16) for h in range(H)]
    vext = [sb(f"vext{c}", (128, H * 65), BF16) for c in range(SBN)]
    stkA = sb("stkA", (128, SBN * CH), BF16)
    stkB = sb("stkB", (128, SBN * CH), BF16)
    projs = stkA[:, 0:DK]
    s_run = sb("s_run", (128, H * 65), F32)
    s_bf = sb("s_bf", (128, H * 65), BF16)
    am = [sb("am0", (128, 512), BF16), None]
    r4 = [sb(f"r4_{i}", (128, 8), F32) for i in range(3)]
    gtmp, gtmp2 = r4[0], r4[1]
    xs = [sb(f"xs{c}", (128, D), F32) for c in range(SBN)]
    x2 = [sb("x2_0", (128, D), F32)]
    xhat = [sb(f"xhat{c}", (128, D), BF16) for c in range(SBN)]
    fm = [sb(f"fm{k}", (128, SBN * CH), BF16) for k in range(KC)]
    atm = [sb(f"atm{c}", (128, D), BF16) for c in range(SBN)]
    h2b = sb("h2b", (128, D), BF16)
    qy = atm[0]
    qy2 = atm[1]
    qb1 = atm[2]
    qb2 = atm[3]
    qsi = h2b
    qst = sb("qst", (128, D), BF16)
    qn = sb("qn", (128, D), I16)
    qsc = sb("qsc", (128, 32), F32)
    qrs = sb("qrs", (128, 32), F32)
    qsb = sb("qsb", (128, 32), F32)
    xdt = xhat
    lst = sb("lst", (128, 4 * 12), F32)
    lmv = sb("lmv", (128, 4 * 2), F32)
    lt1 = sb("lt1", (128, 4), F32)
    lrstd = sb("lrstd", (128, 4), F32)
    lnegmr = sb("lnegmr", (128, 4), F32)
    c_lne = sb("c_lne", (128, 1), F32)
    c_negc0 = sb("c_negc0", (128, 1), F32)
    c_dene = sb("c_dene", (128, 1), F32)
    ident = sb("ident", (128, 128), BF16)
    idf32 = sb("idf32", (128, 128), F32)
    ones_row = sb("ones_row", (1, 128), BF16)
    projC = sb("projC", (128, 128), BF16)
    siluc = sb("siluc", (128, 8), F32)
    modT = sb("modT", (128, 48), F32)
    onca = sb("onca", (128, 8), F32)
    oncb = sb("oncb", (128, 8), F32)
    gacol = sb("gacol", (128, 8), F32)
    gbcol = sb("gbcol", (128, 8), F32)
    biasq = sb("biasq", (128, 8), F32)
    biask = sb("biask", (128, 8), F32)
    b8 = s_run[0:8, 0:128]
    c8 = b8
    mrow = b8
    row_f1 = None
    row_f2 = None
    row_bv = sb("row_bv", (1, D), BF16)
    row_gbo = sb("row_gbo", (1, D), BF16)
    row_gbf = sb("row_gbf", (1, D), BF16)
    row_bb = None
    oncb_b = sb("oncb_b", (128, D), BF16)
    sb_b = sb("sb_b", (128, D), BF16)
    wst = [xs[0], xs[1]]
    wsb = [xhat[0], xhat[1]]
    qf32 = [x2[0][:, 0:512], x2[0][:, 512:1024]]
    am[1] = x2[0].bitcast(BF16)[:, 1024:1536]
    row_f1 = xs[2][0:1, :]
    row_f2 = xs[3][0:1, :]
    row_bb = xhat[2][0:1, :]
    wmod_stage = [
        [qf[i].bitcast(F32) for i in range(16)] + [kf[i].bitcast(F32) for i in range(8)],
        [kf[8 + i].bitcast(F32) for i in range(8)] + [kftm[i].bitcast(F32) for i in range(16)],
    ]
    PA = psum("PA", (128, 1024))
    PB = psum("PB", (128, 1024))
    PT = psum("PT", (128, 512))
    PU = psum("PU", (128, 512))
    PO = psum("PO", (128, 512))
    PK = psum("PK", (128, 512))

    if verbose:
        print("sbuf bytes remaining:", nc.sbuf_bytes_remaining)

    bWT = {w: [Buf(f"{w}T{k}") for k in range(KC)] for w in WT}
    bqf = [Buf(f"qf{h}") for h in range(H)]
    bkf = [Buf(f"kf{h}") for h in range(H)]
    bkftm = [Buf(f"kftm{h}") for h in range(H)]
    bvext = [Buf(f"vext{c}") for c in range(SBN)]
    bstkA, bstkB = Buf("stkA"), Buf("stkB")
    bs_run = [Buf(f"s_run{g}") for g in range(4)]
    bs_bf = [Buf(f"s_bf{g}") for g in range(4)]
    bam = [Buf("am0"), None]
    br4 = [Buf(f"r4_{i}") for i in range(3)]
    bxs = [Buf(f"xs{c}") for c in range(SBN)]
    bx2 = [Buf("x2_0")]
    bxhat = [Buf(f"xhat{c}") for c in range(SBN)]
    bfm = [Buf(f"fm{k}") for k in range(KC)]
    batm = [Buf(f"atm{c}") for c in range(SBN)]
    bh2 = Buf("h2b")
    bqy, bqy2, bqb1, bqb2 = batm[0], batm[1], batm[2], batm[3]
    bqsi = bh2
    bqst, bqn = Buf("qst"), Buf("qn")
    bqsc, bqrs, bqsb = Buf("qsc"), Buf("qrs"), Buf("qsb")
    bxd = bxhat
    bln = [{n: Buf(n + str(i)) for n in ("lst", "lmv", "lt1", "lrstd", "lnegmr")}
           for i in range(4)]
    bconst = Buf("const")
    bmod = Buf("mod")
    bb8 = Buf("b8")
    bmrow = bb8
    brow1, brow2 = bxs[2], bxs[3]
    brow_bv, brow_gbo, brow_gbf = (Buf(x) for x in ("row_bv", "row_gbo", "row_gbf"))
    brow_bb = bxhat[2]
    boncb_b, bsb_b = Buf("oncb_b"), Buf("sb_b")
    bwst = [bxs[0], bxs[1]]
    bwsb = [bxhat[0], bxhat[1]]
    bqf32 = [Buf("qf32_0"), bx2[0]]
    bam[1] = bx2[0]
    bwmod = [Buf("wmod0"), Buf("wmod1")]
    bwmod_alias = [[bqf[i] for i in range(16)] + [bkf[i] for i in range(8)],
                   [bkf[8 + i] for i in range(8)] + [bkftm[i] for i in range(16)]]
    bPA, bPB, bPT, bPU, bPO, bPK = (Buf(x) for x in ("PA", "PB", "PT", "PU", "PO", "PK"))
    bPAq, bPAk, bPBq, bPBk = (Buf(x) for x in ("PAq", "PAk", "PBq", "PBk"))
    bscr = Buf("scratch")
    bout = Buf("out")

    def fp6_round(tag, bsrc_y, sb_bb, bsb_scale):
        E("gpsimd", [lambda e: e.tensor_tensor(qy2, qy, qy, ALU.mult)],
          reads=[bsrc_y], writes=[bqy2])
        E("vector", [lambda e: e.tensor_scalar(qb1, qy2, 4.0, None, ALU.is_ge)],
          reads=[bqy2], writes=[bqb1])
        E("vector", [lambda e: e.tensor_scalar(qb2, qy2, 16.0, None, ALU.is_ge)],
          reads=[bqy2], writes=[bqb2])
        E("vector", [lambda e: e.tensor_scalar(qsi, qb1, -4.0, 8.0, ALU.mult, ALU.add)],
          reads=[bqb1], writes=[bqsi])
        E("vector", [lambda e: e.scalar_tensor_tensor(qsi, qb2, -2.0, qsi,
                                                      ALU.mult, ALU.add)],
          reads=[bqb2, bqsi], writes=[bqsi])
        E("gpsimd", [lambda e: e.tensor_scalar(qst, qb1, 0.125, 0.125,
                                               ALU.mult, ALU.add)],
          reads=[bqb1], writes=[bqst])
        E("vector", [lambda e: e.scalar_tensor_tensor(qst, qb2, 0.25, qst,
                                                      ALU.mult, ALU.add)],
          reads=[bqb2, bqst], writes=[bqst])
        E("vector", [lambda e: e.tensor_tensor(qn, qy, qsi, ALU.mult)],
          reads=[bsrc_y, bqsi], writes=[bqn])
        qsi3 = qsi.rearrange("p (b k) -> p b k", k=32)
        qst3 = qst.rearrange("p (b k) -> p b k", k=32)
        E("vector", [lambda e, qsi3=qsi3, qst3=qst3, sb_bb=sb_bb: e.tensor_tensor(
            qsi3, qst3, sb_bb, ALU.mult)],
          reads=[bqst, bqsb], writes=[bqsi])
        E("vector", [lambda e: e.tensor_tensor(qy, qn, qsi, ALU.mult)],
          reads=[bqn, bqsi], writes=[bqy])

    def quant_head(src_ap, bsrc, scale_col=None, bscale_extra=None):
        s3 = src_ap.rearrange("p (b k) -> p b k", k=32)
        E("vector", [lambda e, s3=s3: e.reduce_max(qsc.unsqueeze(2), s3, axis=AX.X,
                                                   apply_absolute_value=True)],
          reads=[bsrc], writes=[bqsc])
        E("vector", [lambda e: e.tensor_scalar(qsc, qsc, 1.0 / 7.5, 1e-30,
                                               ALU.mult, ALU.max)],
          reads=[bqsc], writes=[bqsc])
        E("vector", [lambda e: e.reciprocal(qrs, qsc)],
          reads=[bqsc], writes=[bqrs])
        if scale_col is not None:
            E("vector", [lambda e: e.tensor_scalar(qsb, qsc, scale_col, None,
                                                   ALU.mult)],
              reads=[bqsc, bscale_extra], writes=[bqsb])
        else:
            E("vector", [lambda e: e.tensor_copy(qsb, qsc)],
              reads=[bqsc], writes=[bqsb])
        rs_b = qrs.unsqueeze(2).broadcast_to((128, 32, 32))
        qy3 = qy.rearrange("p (b k) -> p b k", k=32)
        E("vector", [lambda e, s3=s3, qy3=qy3, rs_b=rs_b: e.tensor_tensor(
            qy3, s3, rs_b, ALU.mult)],
          reads=[bsrc, bqrs], writes=[bqy])
        sb_bb = qsb.unsqueeze(2).broadcast_to((128, 32, 32))
        fp6_round("q", bqy, sb_bb, bqsb)

    def stack_evac(pp_half, bpp, oc, bias_col, bbias, dA, dB, bdA, bdB):
        E("vector", [lambda e: e.tensor_scalar(
            dA[0:64, :], pp_half[0:64, :], bias_col[0:64, oc:oc + 1], None, ALU.add)],
          reads=[bpp, bbias], writes=[bdA])
        E("vector", [lambda e: e.tensor_scalar(
            dB[0:64, :], pp_half[64:128, :], bias_col[64:128, oc:oc + 1], None, ALU.add)],
          reads=[bpp, bbias], writes=[bdB])
        E("scalar", [lambda e: e.activation(
            dA[64:128, :], pp_half[0:64, :], AF.Square, bias=bias_col[0:64, oc:oc + 1])],
          reads=[bpp, bbias], writes=[bdA])
        E("scalar", [lambda e: e.activation(
            dB[64:128, :], pp_half[64:128, :], AF.Square,
            bias=bias_col[64:128, oc:oc + 1])],
          reads=[bpp, bbias], writes=[bdB])
        p.flush("scalar")

    flushbuf = lt1[:, 0:1]
    fdum = sb("fdum", (1, 8), F32)
    bfdum = Buf("fdum")

    fdum2 = sb("fdum2", (1, 8), F32)
    fdum3 = sb("fdum3", (1, 8), F32)
    p.flush_fns = {
        "vector": (lambda e: e.memset(fdum, 0.0), bfdum),
        "scalar": (lambda e: e.memzero(fdum2), Buf("fdum2")),
        "gpsimd": (lambda e: e.memset(fdum3, 0.0), Buf("fdum3")),
    }

    pu_rot = [(PU, bPU), (PK, bPK), (PO, bPO), (PT, bPT)]
    _urot = [0]

    def u_mm(stack, bstack, dst, bdst):
        pu, bpu = pu_rot[_urot[0] % 4]
        fscr, bfscr = qf32[_urot[0] % 2], bqf32[_urot[0] % 2]
        _urot[0] += 1
        E("tensor", [lambda e, pu=pu: e.matmul(pu, projC, stack, start=True, stop=True)],
          reads=[bconst, bstack], writes=[bpu])
        E("scalar", [lambda e, pu=pu, fscr=fscr: e.activation(
            fscr, pu, AF.Exp, bias=c_negc0)],
          reads=[bpu], writes=[bfscr])
        eeng = "vector" if _urot[0] % 2 else "gpsimd"
        E(eeng, [lambda e, fscr=fscr, dst=dst: e.tensor_scalar(
            dst, fscr, PHI_EPS, None, ALU.add)],
          reads=[bfscr], writes=[bdst])

    def fm_to_row(col_ap, breads, dst_row, bdst):
        E("tensor", [lambda e: e.matmul(PT[0:8, 0:128], col_ap, idf32,
                                        start=True, stop=True)],
          reads=breads + [bconst], writes=[bPT])
        E("vector", [lambda e: e.tensor_copy(mrow, PT[0:8, 0:128])],
          reads=[bPT], writes=[bmrow])
        E("scalar", [lambda e: e.dma_start(out=scr, in_=mrow)],
          reads=[bmrow], writes=[bscr], dma=True)
        E("scalar", [lambda e: e.dma_start(out=dst_row,
                                           in_=scr.rearrange("a b -> (a b)").unsqueeze(0))],
          reads=[bscr], writes=[bdst], dma=True)

    E("gpsimd", [lambda e: e.memset(ident, 1.0)], writes=[bconst])
    E("gpsimd", [lambda e: e.affine_select(ident, ident, [[-1, 128]], ALU.is_equal,
                                           0.0, base=0, channel_multiplier=1)],
      reads=[bconst], writes=[bconst])
    E("gpsimd", [lambda e: e.memset(idf32, 1.0)], writes=[bconst])
    E("gpsimd", [lambda e: e.affine_select(idf32, idf32, [[-1, 128]], ALU.is_equal,
                                           0.0, base=0, channel_multiplier=1)],
      reads=[bconst], writes=[bconst])
    E("vector", [lambda e: e.memset(ones_row, 1.0)], writes=[bconst])
    E("vector", [lambda e: e.memset(c_lne, LN_EPS)], writes=[bconst])
    E("vector", [lambda e: e.memset(c_negc0, -C0)], writes=[bconst])
    E("vector", [lambda e: e.memset(c_dene, DEN_EPS)], writes=[bconst])
    for c in range(SBN):
        vv = vext[c].rearrange("p (h n) -> p h n", n=65)[:, :, 64:65]
        E("vector", [lambda e, vv=vv: e.memset(vv, 1.0)], writes=[bvext[c]])

    E("sync", [lambda e: e.dma_start(out=c8, in_=ch.rearrange("(j p) -> j p", p=128))],
      writes=[bb8], dma=True)
    E("tensor", [lambda e: e.matmul(PT[:, 0:8], c8, idf32[0:8, 0:8],
                                    start=True, stop=True)],
      reads=[bb8, bconst], writes=[bPT])
    E("scalar", [lambda e: e.activation(siluc, PT[:, 0:8], AF.Silu)],
      reads=[bPT], writes=[bmod])

    E("scalar", [lambda e: e.dma_start(out=wst[0][:, 0:DK], in_=projh)],
      writes=[bwst[0]], dma=True)
    E("vector", [lambda e: e.tensor_scalar(projs, wst[0][:, 0:DK], S_XN, None,
                                           ALU.mult)],
      reads=[bwst[0]], writes=[bstkA])
    E("tensor", [lambda e: e.matmul(PT[0:DK, 0:128], projs, ident,
                                    start=True, stop=True)],
      reads=[bstkA, bconst], writes=[bPT])
    E("vector", [lambda e: e.tensor_copy(projC[0:DK, :], PT[0:DK, 0:128])],
      reads=[bPT], writes=[bconst])
    E("vector", [lambda e: e.memset(projC[DK:128, :], SQC)], writes=[bconst])

    for (hb, dst) in ((bh["bq"], biasq), (bh["bk"], biask)):
        E("scalar", [lambda e, hb=hb: e.dma_start(
            out=b8, in_=hb.rearrange("(j p) -> j p", p=128))],
          writes=[bb8], dma=True)
        E("tensor", [lambda e: e.matmul(PT[:, 8:16], b8, idf32[0:8, 0:8],
                                        start=True, stop=True)],
          reads=[bb8, bconst], writes=[bPT])
        E("vector", [lambda e, dst=dst: e.tensor_copy(dst, PT[:, 8:16])],
          reads=[bPT], writes=[bmod])

    E("scalar", [lambda e: e.dma_start(out=row_f1, in_=bh["bv"].unsqueeze(0))],
      writes=[brow1], dma=True)
    E("vector", [lambda e: e.tensor_copy(row_bv, row_f1)],
      reads=[brow1], writes=[brow_bv])

    def weight_prep(wname, scale_col=None, quantize=False):
        src = wh[wname].rearrange("(n p) k -> n p k", p=128)
        for oc in range(KC):
            st, bst = wst[oc % 2], bwst[oc % 2]
            cb, bcb = wsb[oc % 2], bwsb[oc % 2]
            q = "gpsimd" if oc % 2 else "scalar"
            E(q, [lambda e, oc=oc, st=st: e.dma_start(out=st, in_=src[oc])],
              writes=[bst], dma=True)
            if quantize:
                quant_head(st, bst, scale_col=scale_col[:, oc:oc + 1]
                           if scale_col is not None else None, bscale_extra=bmod)
                E("vector", [lambda e, cb=cb: e.tensor_copy(cb, qy)],
                  reads=[bqy], writes=[bcb])
            elif scale_col is not None:
                E("vector", [lambda e, st=st, cb=cb, oc=oc: e.tensor_scalar(
                    cb, st, scale_col[:, oc:oc + 1], None, ALU.mult)],
                  reads=[bst, bmod], writes=[bcb])
            else:
                E("vector", [lambda e, st=st, cb=cb: e.tensor_copy(cb, st)],
                  reads=[bst], writes=[bcb])
            for half in range(2):
                def mmz(e, cb=cb, half=half):
                    ins = None
                    for i in range(4):
                        kcc = half * 4 + i
                        ins = e.matmul(PT[:, i * 128:(i + 1) * 128],
                                       cb[:, kcc * 128:(kcc + 1) * 128],
                                       ident, start=True, stop=True)
                    return ins
                E("tensor", [mmz], reads=[bcb, bconst], writes=[bPT])

                def ev(e, wname=wname, oc=oc, half=half):
                    ins = None
                    for i in range(4):
                        kcc = half * 4 + i
                        ins = e.tensor_copy(
                            WT[wname][kcc][:, oc * 128:(oc + 1) * 128],
                            PT[:, i * 128:(i + 1) * 128])
                    return ins
                E("vector", [ev], reads=[bPT],
                  writes=[bWT[wname][half * 4 + i] for i in range(4)])

    weight_prep("wq")
    weight_prep("wk")
    weight_prep("wv")

    wm = wmodh.rearrange("(n p) j -> n p j", p=128)
    for kcc in range(KC):
        j = kcc % 2
        stg = wmod_stage[j]
        E("sync", [(lambda e, kcc=kcc, i=i, stg=stg: e.dma_start(
            out=stg[i], in_=wm[kcc, :, i * 256:(i + 1) * 256]))
            for i in range(24)],
          writes=[bwmod[j]] + bwmod_alias[j], dma=True)

        def modmm(e, kcc=kcc, stg=stg):
            ins = None
            for jt in range(48):
                seg, off = divmod(jt * 128, 256)
                ins = e.matmul(PU[:, jt:jt + 1], stg[seg][:, off:off + 128],
                               siluc[:, kcc:kcc + 1],
                               start=(kcc == 0 and jt == 0),
                               stop=(kcc == KC - 1 and jt == 47),
                               skip_group_check=True)
            return ins
        E("tensor", [modmm], reads=[bwmod[j], bmod], writes=[bPU])
    E("vector", [lambda e: e.tensor_copy(modT, PU[:, 0:48])],
      reads=[bPU], writes=[bmod])

    E("vector", [lambda e: e.tensor_scalar(onca, modT[:, 8:16], 1.0, None, ALU.add)],
      reads=[bmod], writes=[bmod])
    E("vector", [lambda e: e.tensor_scalar(oncb, modT[:, 32:40], 1.0, None, ALU.add)],
      reads=[bmod], writes=[bmod])
    for (sl0, sl1, dst) in ((16, 24, gacol), (40, 48, gbcol)):
        E("vector", [lambda e, sl0=sl0, sl1=sl1: e.tensor_scalar(
            gtmp, modT[:, sl0:sl1], 0.0, 2.0, ALU.is_ge, ALU.mult)],
          reads=[bmod], writes=[bmod, br4[0]])
        E("vector", [lambda e: e.tensor_scalar(gtmp, gtmp, -1.0, None, ALU.add)],
          reads=[bmod], writes=[bmod])
        E("scalar", [lambda e, sl0=sl0, sl1=sl1: e.activation(
            gtmp2, modT[:, sl0:sl1], AF.Abs)],
          reads=[bmod], writes=[bmod, br4[1]])
        E("vector", [lambda e: e.tensor_scalar(gtmp2, gtmp2, 0.25, None, ALU.max)],
          reads=[bmod], writes=[bmod])
        E("vector", [lambda e, dst=dst: e.tensor_tensor(dst, gtmp, gtmp2, ALU.mult)],
          reads=[bmod], writes=[bmod])

    for (colap, dstt, bd) in ((oncb, oncb_b, boncb_b), (modT[:, 24:32], sb_b, bsb_b)):
        fm_to_row(colap, [bmod], row_f1, brow1)
        E("vector", [lambda e: e.tensor_copy(row_bb, row_f1)],
          reads=[brow1], writes=[brow_bb])
        for half in range(2):
            E("tensor", [lambda e, half=half: e.matmul(
                PT[:, 0:512], ones_row, row_bb[:, half * 512:(half + 1) * 512],
                start=True, stop=True)],
              reads=[brow_bb, bconst], writes=[bPT])
            E("vector", [lambda e, half=half, dstt=dstt: e.tensor_copy(
                dstt[:, half * 512:(half + 1) * 512], PT[:, 0:512])],
              reads=[bPT], writes=[bd])

    for (gcol, bname, dst, bd) in ((gacol, "bo", row_gbo, brow_gbo),
                                   (gbcol, "bf", row_gbf, brow_gbf)):
        fm_to_row(gcol, [bmod], row_f1, brow1)
        E("scalar", [lambda e, bname=bname: e.dma_start(
            out=row_f2, in_=bh[bname].unsqueeze(0))],
          writes=[brow2], dma=True)
        E("vector", [lambda e, dst=dst: e.tensor_tensor(dst, row_f1, row_f2,
                                                        ALU.mult)],
          reads=[brow1, brow2], writes=[bd])

    E("gpsimd", [lambda e: e.memset(idf32, 1.0)], writes=[bconst])
    E("gpsimd", [lambda e: e.affine_select(idf32, idf32, [[1, 128]], ALU.is_ge,
                                           0.0, base=0, channel_multiplier=-1)],
      reads=[bconst], writes=[bconst])

    weight_prep("wo", scale_col=gacol)
    weight_prep("wf", scale_col=gbcol, quantize=True)

    E("vector", [lambda e: e.memset(s_run, 0.0)], reads=[bb8], writes=bs_run)
    E("vector", [lambda e: e.memset(s_bf, 0.0)], writes=bs_bf)

    pa_rot = [(PA, bPAq, bPAk), (PB, bPBq, bPBk)]
    pt_rot = [(PT, bPT), (PU, bPU)]
    out_tokens = []
    out3 = outh.rearrange("(n p) d -> n p d", p=128)

    def ln_chain(src_ap, bsrc, dst_ap, bdst, ci, feng="scalar"):
        L = bln[ci]
        _lst = lst[:, ci * 12:(ci + 1) * 12]
        _lmv = lmv[:, ci * 2:(ci + 1) * 2]
        _lt1 = lt1[:, ci:ci + 1]
        _lrstd = lrstd[:, ci:ci + 1]
        _lnegmr = lnegmr[:, ci:ci + 1]
        E("vector", [lambda e: e.bn_stats(_lst[:, 0:6], src_ap[:, 0:512]),
                     lambda e: e.bn_stats(_lst[:, 6:12], src_ap[:, 512:1024])],
          reads=[bsrc], writes=[L["lst"]])
        E("vector", [lambda e: e.bn_aggr(_lmv, _lst)],
          reads=[L["lst"]], writes=[L["lmv"]], strict=True)
        E("scalar", [lambda e: e.activation(_lt1, _lmv[:, 1:2], AF.Ln, bias=c_lne)],
          reads=[L["lmv"]], writes=[L["lt1"]])
        E("scalar", [lambda e: e.activation(_lrstd, _lt1, AF.Exp, scale=-0.5)],
          reads=[L["lt1"]], writes=[L["lrstd"]])
        E("vector", [lambda e: e.tensor_scalar(_lnegmr, _lmv[:, 0:1], _lrstd, -1.0,
                                               ALU.mult, ALU.mult)],
          reads=[L["lmv"], L["lrstd"]], writes=[L["lnegmr"]])
        if feng == "scalar":
            E("scalar", [lambda e: e.activation(dst_ap, src_ap, AF.Identity,
                                                bias=_lnegmr, scale=_lrstd)],
              reads=[bsrc, L["lrstd"], L["lnegmr"]], writes=[bdst])
        else:
            E("vector", [lambda e: e.tensor_scalar(dst_ap, src_ap, _lrstd,
                                                   _lnegmr, ALU.mult, ALU.add)],
              reads=[bsrc, L["lrstd"], L["lnegmr"]], writes=[bdst])
            p.flush("vector")

    st_sets = [
        (stkA, stkB, atm[0][:, 0:512], atm[0][:, 512:1024],
         bstkA, bstkB, batm[0], batm[0]),
        (atm[1][:, 0:512], atm[1][:, 512:1024],
         atm[2][:, 0:512], atm[2][:, 512:1024],
         batm[1], batm[1], batm[2], batm[2]),
    ]
    maskb = idf32.unsqueeze(1).broadcast_to((128, 4, 128))
    po_rot3 = [(PO, bPO), (PB[:, 512:1024], bPBk), (PA[:, 512:1024], bPAk)]
    pu_rot2 = [(PU, bPU), (PB[:, 0:512], bPBq), (PT, bPT)]
    pk_rot = [(PK, bPK), (PA[:, 0:512], bPAq)]

    def smalls_head(idx):
        c, g = divmod(idx, 4)
        hs = [4 * g + i for i in range(4)]
        POr, bPOr = po_rot3[idx % 3]
        PUr, bPUr = pu_rot2[idx % 3]
        ami, bami = am[idx % 2], bam[idx % 2]

        def inter_mm(e, hs=hs, c=c, POr=POr):
            ins = None
            for i, h in enumerate(hs):
                ins = e.matmul(POr[:, i * 65:(i + 1) * 65],
                               qf[h][:, c * 128:(c + 1) * 128],
                               s_bf[:, h * 65:(h + 1) * 65],
                               start=(i == 0), stop=False,
                               skip_group_check=True)
            return ins
        E("tensor", [inter_mm],
          reads=[bqf[h] for h in hs] + [bs_bf[g]], writes=[bPOr])

        def a_mm(e, hs=hs, c=c, PUr=PUr):
            ins = None
            for i, h in enumerate(hs):
                ins = e.matmul(PUr[:, i * 128:(i + 1) * 128],
                               kf[h][:, c * 128:(c + 1) * 128],
                               qf[h][:, c * 128:(c + 1) * 128],
                               start=(i == 0), stop=(i == 3))
            return ins
        E("tensor", [a_mm],
          reads=[bqf[h] for h in hs] + [bkf[h] for h in hs], writes=[bPUr])
        ami3 = ami.rearrange("p (h c) -> p h c", c=128)
        pu3 = PUr.rearrange("p (h c) -> p h c", c=128)
        E("vector", [lambda e, ami3=ami3, pu3=pu3: e.tensor_tensor(
            ami3, pu3, maskb, ALU.mult)],
          reads=[bPUr, bconst], writes=[bami])
        p.flush("vector")
        return (idx, c, g, hs, POr, bPOr, ami, bami)

    def smalls_tail(st):
        idx, c, g, hs, POr, bPOr, ami, bami = st
        PKr, bPKr = pk_rot[idx % 2]
        rr, brr = r4[idx % 3], br4[idx % 3]

        def intra_mm(e, hs=hs, c=c, ami=ami, POr=POr):
            ins = None
            for i, h in enumerate(hs):
                ins = e.matmul(POr[:, i * 65:(i + 1) * 65],
                               ami[:, i * 128:(i + 1) * 128],
                               vext[c][:, h * 65:(h + 1) * 65],
                               start=False, stop=(i == 3),
                               skip_group_check=True)
            return ins
        E("tensor", [intra_mm], reads=[bami, bvext[c]], writes=[bPOr])

        def kv_mm(e, hs=hs, c=c, PKr=PKr):
            ins = None
            for i, h in enumerate(hs):
                ins = e.matmul(PKr[:, i * 65:(i + 1) * 65],
                               kftm[h][:, c * 128:(c + 1) * 128],
                               vext[c][:, h * 65:(h + 1) * 65],
                               start=(i == 0), stop=(i == 3),
                               skip_group_check=True)
            return ins
        E("tensor", [kv_mm],
          reads=[bkftm[h] for h in hs] + [bvext[c]], writes=[bPKr])
        E("vector", [lambda e, g=g, PKr=PKr: e.tensor_tensor(
            s_run[:, g * 260:(g + 1) * 260],
            s_run[:, g * 260:(g + 1) * 260], PKr[:, 0:260], ALU.add)],
          reads=[bPKr, bs_run[g]], writes=[bs_run[g]])
        E("scalar", [lambda e, g=g: e.activation(
            s_bf[:, g * 260:(g + 1) * 260],
            s_run[:, g * 260:(g + 1) * 260], AF.Identity)],
          reads=[bs_run[g]], writes=[bs_bf[g]])
        p.flush("scalar")
        po3 = POr[:, 0:260].rearrange("p (h n) -> p h n", n=65)
        E("scalar", [lambda e, po3=po3, rr=rr: e.activation(
            rr[:, 4:8].unsqueeze(2), po3[:, :, 64:65], AF.Ln, bias=c_dene)],
          reads=[bPOr, bconst], writes=[brr])
        E("scalar", [lambda e, rr=rr: e.activation(
            rr[:, 0:4], rr[:, 4:8], AF.Exp, scale=-1.0)],
          reads=[brr], writes=[brr])
        return (c, g, po3, bPOr, rr, brr)

    def smalls_at3(t2):
        c, g, po3, bPOr, rr, brr = t2
        at3 = atm[c].rearrange("p (h d) -> p h d", d=64)[:, 4 * g:4 * g + 4, :]
        E("vector", [lambda e, po3=po3, rr=rr, at3=at3: e.tensor_tensor(
            at3, po3[:, :, 0:64],
            rr[:, 0:4].unsqueeze(2).broadcast_to((128, 4, 64)), ALU.mult)],
          reads=[bPOr, brr], writes=[batm[c]])
        p.flush("vector")

    for sbi in range(NSB):
        for c in range(SBN):
            gc = sbi * SBN + c
            E("sync", [lambda e, gc=gc, c=c: e.dma_start(
                out=xs[c], in_=xh.rearrange("(n p) d -> n p d", p=128)[gc])],
              writes=[bxs[c]], dma=True)
            ln_chain(xs[c], bxs[c], xhat[c], bxhat[c], c, feng="vector")
            for half in range(2):
                PTr, bPTr = pt_rot[half]

                def tmm(e, c=c, half=half, PTr=PTr):
                    ins = None
                    for i in range(4):
                        kcc = half * 4 + i
                        ins = e.matmul(PTr[:, i * 128:(i + 1) * 128],
                                       xhat[c][:, kcc * 128:(kcc + 1) * 128],
                                       ident, start=True, stop=True)
                    return ins
                E("tensor", [tmm], reads=[bxhat[c], bconst], writes=[bPTr])

                def fold(e, c=c, half=half, PTr=PTr):
                    ins = None
                    for i in range(4):
                        kcc = half * 4 + i
                        ins = e.tensor_scalar(
                            fm[kcc][:, c * 128:(c + 1) * 128],
                            PTr[:, i * 128:(i + 1) * 128],
                            onca[:, kcc:kcc + 1], modT[:, kcc:kcc + 1],
                            ALU.mult, ALU.add)
                    return ins
                E("vector", [fold], reads=[bPTr, bmod],
                  writes=[bfm[half * 4 + i] for i in range(4)])
            p.flush("vector")

        def u_mm4(po):
            sA, sB, sC, sD, bA, bB, bC, bD = st_sets[po % 2]
            u_mm(sA, bA, qf[po * 2], bqf[po * 2])
            u_mm(sB, bB, qf[po * 2 + 1], bqf[po * 2 + 1])
            u_mm(sC, bC, kf[po * 2], bkf[po * 2])
            u_mm(sD, bD, kf[po * 2 + 1], bkf[po * 2 + 1])

        pend_u = None
        for oc in range(KC):
            pp, bppq, bppk = pa_rot[oc % 2]
            sA, sB, sC, sD, bA, bB, bC, bD = st_sets[oc % 2]

            def qmm(e, oc=oc, pp=pp):
                ins = None
                for kcc in range(KC):
                    ins = e.matmul(pp[:, 0:512],
                                   WT["wq"][kcc][:, oc * 128:(oc + 1) * 128],
                                   fm[kcc], start=(kcc == 0), stop=(kcc == KC - 1))
                return ins
            E("tensor", [qmm], reads=bfm + bWT["wq"], writes=[bppq])

            def kmm(e, oc=oc, pp=pp):
                ins = None
                for kcc in range(KC):
                    ins = e.matmul(pp[:, 512:1024],
                                   WT["wk"][kcc][:, oc * 128:(oc + 1) * 128],
                                   fm[kcc], start=(kcc == 0), stop=(kcc == KC - 1))
                return ins
            E("tensor", [kmm], reads=bfm + bWT["wk"], writes=[bppk])
            if pend_u is not None:
                u_mm4(pend_u)
            stack_evac(pp[:, 0:512], bppq, oc, biasq, bmod, sA, sB, bA, bB)
            stack_evac(pp[:, 512:1024], bppk, oc, biask, bmod, sC, sD, bC, bD)
            pend_u = oc
        u_mm4(pend_u)
        p.flush("vector")
        p.flush("gpsimd")

        for c in range(SBN):
            pp, bppq, bppk = pa_rot[c % 2]

            def vmm(e, c=c, pp=pp):
                ins = None
                for half in range(2):
                    for kcc in range(KC):
                        ins = e.matmul(pp[:, half * 512:(half + 1) * 512],
                                       fm[kcc][:, c * 128:(c + 1) * 128],
                                       WT["wv"][kcc][:, half * 512:(half + 1) * 512],
                                       start=(kcc == 0), stop=False)
                    ins = e.matmul(pp[:, half * 512:(half + 1) * 512], ones_row,
                                   row_bv[:, half * 512:(half + 1) * 512],
                                   start=False, stop=True)
                return ins
            E("tensor", [vmm], reads=bfm + bWT["wv"] + [brow_bv, bconst],
              writes=[bppq, bppk])
            vv = vext[c].rearrange("p (h n) -> p h n", n=65)[:, :, 0:64]
            E("vector", [lambda e, pp=pp, vv=vv: e.tensor_copy(
                vv, pp.rearrange("p (h d) -> p h d", d=64))],
              reads=[bppq, bppk], writes=[bvext[c]])
            p.flush("vector")

        for h in range(H):
            PTr, bPTr = pt_rot[h % 2]

            def tmm(e, h=h, PTr=PTr):
                ins = None
                for c in range(SBN):
                    ins = e.matmul(PTr[:, c * 128:(c + 1) * 128],
                                   kf[h][:, c * 128:(c + 1) * 128], ident,
                                   start=True, stop=True)
                return ins
            E("tensor", [tmm], reads=[bkf[h], bconst], writes=[bPTr])
            E("vector", [lambda e, h=h, PTr=PTr: e.tensor_copy(kftm[h], PTr)],
              reads=[bPTr], writes=[bkftm[h]])
        p.flush("vector")

        pend = None
        pend2 = None
        for c in range(SBN):
            for g in range(4):
                st = smalls_head(c * 4 + g)
                if pend is not None:
                    t2 = smalls_tail(pend)
                    if pend2 is not None:
                        smalls_at3(pend2)
                    pend2 = t2
                pend = st
        t2 = smalls_tail(pend)
        smalls_at3(pend2)
        smalls_at3(t2)

        for kcc in range(KC):
            PTr, bPTr = pt_rot[kcc % 2]

            def tmm(e, kcc=kcc, PTr=PTr):
                ins = None
                for c in range(SBN):
                    ins = e.matmul(PTr[:, c * 128:(c + 1) * 128],
                                   atm[c][:, kcc * 128:(kcc + 1) * 128], ident,
                                   start=True, stop=True)
                return ins
            E("tensor", [tmm], reads=batm + [bconst], writes=[bPTr])
            if kcc % 2 == 0:
                E("vector", [lambda e, kcc=kcc, PTr=PTr: e.tensor_copy(
                    fm[kcc], PTr)],
                  reads=[bPTr], writes=[bfm[kcc]])
            else:
                E("scalar", [lambda e, kcc=kcc, PTr=PTr: e.activation(
                    fm[kcc], PTr, AF.Identity)],
                  reads=[bPTr], writes=[bfm[kcc]])
        p.flush("vector")
        p.flush("scalar")

        for c in range(SBN):
            gc = sbi * SBN + c
            pp, bppq, bppk = pa_rot[c % 2]

            def womm(e, c=c, pp=pp):
                ins = None
                for half in range(2):
                    for kcc in range(KC):
                        ins = e.matmul(pp[:, half * 512:(half + 1) * 512],
                                       fm[kcc][:, c * 128:(c + 1) * 128],
                                       WT["wo"][kcc][:, half * 512:(half + 1) * 512],
                                       start=(kcc == 0), stop=False)
                    ins = e.matmul(pp[:, half * 512:(half + 1) * 512], ones_row,
                                   row_gbo[:, half * 512:(half + 1) * 512],
                                   start=False, stop=True)
                return ins
            E("tensor", [womm], reads=bfm + bWT["wo"] + [brow_gbo, bconst],
              writes=[bppq, bppk])
            E("vector", [lambda e, c=c, pp=pp: e.tensor_tensor(
                xs[c], xs[c], pp, ALU.add)],
              reads=[bppq, bppk, bxs[c]], writes=[bxs[c]])
            ln_chain(xs[c], bxs[c], h2b, bh2, c)
            E("vector", [lambda e: e.tensor_tensor(h2b, h2b, oncb_b, ALU.mult)],
              reads=[bh2, boncb_b], writes=[bh2])
            E("vector", [lambda e: e.tensor_tensor(h2b, h2b, sb_b, ALU.add)],
              reads=[bh2, bsb_b], writes=[bh2])
            quant_head(h2b, bh2)
            E("vector", [lambda e, c=c: e.tensor_copy(xdt[c], qy)],
              reads=[bqy], writes=[bxd[c]])
            p.flush("vector")

        for c in range(SBN):
            for half in range(2):
                PTr, bPTr = pt_rot[half]

                def tmm(e, c=c, half=half, PTr=PTr):
                    ins = None
                    for i in range(4):
                        kcc = half * 4 + i
                        ins = e.matmul(PTr[:, i * 128:(i + 1) * 128],
                                       xdt[c][:, kcc * 128:(kcc + 1) * 128],
                                       ident, start=True, stop=True)
                    return ins
                E("tensor", [tmm], reads=[bxd[c], bconst], writes=[bPTr])
                if half == 0:
                    def fmc(e, c=c, half=half, PTr=PTr):
                        ins = None
                        for i in range(4):
                            kcc = half * 4 + i
                            ins = e.tensor_copy(
                                fm[kcc][:, c * 128:(c + 1) * 128],
                                PTr[:, i * 128:(i + 1) * 128])
                        return ins
                    E("vector", [fmc], reads=[bPTr],
                      writes=[bfm[half * 4 + i] for i in range(4)])
                else:
                    def fmc(e, c=c, half=half, PTr=PTr):
                        ins = None
                        for i in range(4):
                            kcc = half * 4 + i
                            ins = e.activation(
                                fm[kcc][:, c * 128:(c + 1) * 128],
                                PTr[:, i * 128:(i + 1) * 128], AF.Identity)
                        return ins
                    E("scalar", [fmc], reads=[bPTr],
                      writes=[bfm[half * 4 + i] for i in range(4)])
        p.flush("vector")
        p.flush("scalar")
        for c in range(SBN):
            gc = sbi * SBN + c
            pp, bppq, bppk = pa_rot[c % 2]

            def ffmm(e, c=c, pp=pp):
                ins = None
                for half in range(2):
                    for kcc in range(KC):
                        ins = e.matmul(pp[:, half * 512:(half + 1) * 512],
                                       fm[kcc][:, c * 128:(c + 1) * 128],
                                       WT["wf"][kcc][:, half * 512:(half + 1) * 512],
                                       start=(kcc == 0), stop=False)
                    ins = e.matmul(pp[:, half * 512:(half + 1) * 512], ones_row,
                                   row_gbf[:, half * 512:(half + 1) * 512],
                                   start=False, stop=True)
                return ins
            E("tensor", [ffmm], reads=bfm + bWT["wf"] + [brow_gbf, bconst],
              writes=[bppq, bppk])
            E("vector", [lambda e, c=c, pp=pp: e.tensor_tensor(
                xs[c], xs[c], pp, ALU.add)],
              reads=[bppq, bppk, bxs[c]], writes=[bxs[c]])
            tok = E("sync", [lambda e, gc=gc, c=c: e.dma_start(
                out=out3[gc], in_=xs[c])],
                reads=[bxs[c]], writes=[bout], dma=True)
            out_tokens.append(tok)

    p.finalize(out_tokens[-1:] if out_tokens else [])
    if verbose:
        print("groups:", p.n_groups)
    return nc


def kernel(L0_attn_x, L0_attn_Wq, L0_attn_bq, L0_attn_Wk, L0_attn_bk,
           L0_attn_Wv, L0_attn_bv, L0_attn_Wo, L0_attn_bo, L0_attn_proj,
           L0_ffn_weight, L0_ffn_bias, L0_c, L0_wmod):
    x = np.ascontiguousarray(np.asarray(L0_attn_x, dtype=np.float32))
    nc = build_core_program()
    common = {
        "wq": np.ascontiguousarray(L0_attn_Wq, dtype=np.float32),
        "wk": np.ascontiguousarray(L0_attn_Wk, dtype=np.float32),
        "wv": np.ascontiguousarray(L0_attn_Wv, dtype=np.float32),
        "wo": np.ascontiguousarray(L0_attn_Wo, dtype=np.float32),
        "wf": np.ascontiguousarray(L0_ffn_weight, dtype=np.float32),
        "bq": np.ascontiguousarray(L0_attn_bq, dtype=np.float32),
        "bk": np.ascontiguousarray(L0_attn_bk, dtype=np.float32),
        "bv": np.ascontiguousarray(L0_attn_bv, dtype=np.float32),
        "bo": np.ascontiguousarray(L0_attn_bo, dtype=np.float32),
        "bf": np.ascontiguousarray(L0_ffn_bias, dtype=np.float32),
        "proj": np.ascontiguousarray(L0_attn_proj, dtype=np.float32),
        "wmod": np.ascontiguousarray(L0_wmod, dtype=np.float32),
    }
    in_maps = []
    for b in range(8):
        m = dict(common)
        m["x"] = np.ascontiguousarray(x[b])
        m["c"] = np.ascontiguousarray(np.asarray(L0_c, dtype=np.float32)[b])
        in_maps.append(m)
    res = run_bass_kernel_spmd(nc, in_maps, core_ids=list(range(8)))
    return np.stack([res.results[b]["out"] for b in range(8)], axis=0)


# version 13 (final)
# speedup vs baseline: 1.1164x; 1.1164x over previous
import math
import numpy as np
import concourse.bass as bass
import concourse.mybir as mybir
from concourse.bass_utils import run_bass_kernel_spmd

F32 = mybir.dt.float32
BF16 = mybir.dt.bfloat16
I16 = mybir.dt.int16
AF = mybir.ActivationFunctionType
ALU = mybir.AluOpType
AX = mybir.AxisListType

T = 4096
D = 1024
H = 16
DK = 64
PM = 128
CH = 128
NCH = T // CH
SBN = 4
NSB = NCH // SBN
KC = D // 128
S_XN = 64.0 ** -0.25
SQC = -0.0625
C0 = 0.5 * math.log(128.0)
LN_EPS = 1e-5
PHI_EPS = 1e-6
DEN_EPS = 1e-6

STAGE = "full"


class Buf:
    __slots__ = ("writer", "readers", "name")

    def __init__(self, name=""):
        self.writer = None
        self.readers = []
        self.name = name


class Prog:
    QUEUES = ("sync", "scalar", "gpsimd")

    def __init__(self, nc):
        self.nc = nc
        self.plan = {e: [] for e in ("sync", "tensor", "vector", "scalar", "gpsimd")}
        self.count = {e: 0 for e in ("tensor", "vector", "scalar", "gpsimd")}
        for q in self.QUEUES:
            self.count["dma_" + q] = 0
        self.known = {e: {} for e in ("sync", "tensor", "vector", "scalar", "gpsimd")}
        self.hist = {s: [] for s in self.count}
        self.n_groups = 0
        self.flush_fns = {}
        self.flush_hist = {e: [] for e in ("vector", "scalar", "gpsimd")}

    def flush(self, eng):
        ffn, fbuf = self.flush_fns[eng]
        cov = self.count[eng]
        ftok = self.emit(eng, [ffn], writes=[fbuf])
        self.flush_hist[eng].append((cov, ftok[1]))
        return ftok

    def _dep(self, eng, token, waits, strict=False):
        if token is None:
            return
        sem, val = token
        if sem == eng and not strict:
            return
        if self.known[eng].get(sem, 0) >= val:
            return
        waits[sem] = max(waits.get(sem, 0), val)

    def emit(self, eng, fns, reads=(), writes=(), dma=False, strict=True):
        assert not dma or eng in self.QUEUES
        waits = {}
        for b in reads:
            self._dep(eng, b.writer, waits, strict)
        for b in writes:
            self._dep(eng, b.writer, waits, strict)
            for r in b.readers:
                self._dep(eng, r, waits, strict)
        for sem, val in waits.items():
            kn = self.known[eng]
            kn[sem] = max(kn.get(sem, 0), val)
            snap = None
            for v, cl in reversed(self.hist[sem]):
                if v <= val:
                    snap = cl
                    break
            if snap:
                for s2, v2 in snap.items():
                    if kn.get(s2, 0) < v2:
                        kn[s2] = v2
        if eng == "tensor" and self.flush_fns:
            for s in list(waits):
                if s in ("vector", "scalar", "gpsimd"):
                    val = waits[s]
                    del waits[s]
                    tokv = None
                    for (cov, fv) in self.flush_hist[s]:
                        if cov >= val:
                            tokv = fv
                            break
                    if tokv is None:
                        ffn, fbuf = self.flush_fns[s]
                        cov = self.count[s]
                        ftok = self.emit(s, [ffn], writes=[fbuf])
                        tokv = ftok[1]
                        self.flush_hist[s].append((cov, tokv))
                    waits[s] = tokv
                    self.known[eng][s] = max(self.known[eng].get(s, 0), tokv)
        sem = ("dma_" + eng) if dma else eng
        self.count[sem] += 16 * len(fns) if dma else 1
        val = self.count[sem]
        self.plan[eng].append((dict(waits), list(fns), sem, dma))
        token = (sem, val)
        snap = dict(self.known[eng])
        snap[sem] = val
        self.hist[sem].append((val, snap))
        for b in writes:
            b.writer = token
            b.readers = []
        for b in reads:
            b.readers.append(token)
        self.n_groups += 1
        return token

    def finalize(self, final_tokens):
        nc = self.nc
        sem_handles = {}
        ctxs = []
        for s in self.count:
            cm = nc.semaphore("sem_" + s)
            sem_handles[s] = cm.__enter__()
            ctxs.append(cm)
        blk_cm = nc.Block()
        block = blk_cm.__enter__()
        try:
            def make_body(ename):
                entries = self.plan[ename]

                def body(eng):
                    for waits, fns, sem, dma in entries:
                        for ws in sorted(waits):
                            eng.wait_ge(sem_handles[ws], waits[ws])
                        for i, f in enumerate(fns):
                            inst = f(eng)
                            if dma:
                                inst.then_inc(sem_handles[sem], 16)
                            elif i == len(fns) - 1:
                                inst.then_inc(sem_handles[sem], 1)
                    if ename == "sync":
                        for s2, v2 in final_tokens:
                            eng.wait_ge(sem_handles[s2], v2)
                return body

            for name, deco in (("sync", block.sync), ("tensor", block.tensor),
                               ("vector", block.vector), ("scalar", block.scalar),
                               ("gpsimd", block.gpsimd)):
                deco(make_body(name))
        finally:
            blk_cm.__exit__(None, None, None)
            for cm in ctxs:
                cm.__exit__(None, None, None)


def build_core_program(verbose=False):
    nc = bass.Bass("TRN2", target_bir_lowering=False)
    p = Prog(nc)
    E = p.emit

    xh = nc.dram_tensor("x", [T, D], F32, kind="ExternalInput").ap()
    ch = nc.dram_tensor("c", [D], F32, kind="ExternalInput").ap()
    wmodh = nc.dram_tensor("wmod", [D, 6 * D], F32, kind="ExternalInput").ap()
    wh = {}
    for nm in ("wq", "wk", "wv", "wo", "wf"):
        wh[nm] = nc.dram_tensor(nm, [D, D], F32, kind="ExternalInput").ap()
    bh = {}
    for nm in ("bq", "bk", "bv", "bo", "bf"):
        bh[nm] = nc.dram_tensor(nm, [D], F32, kind="ExternalInput").ap()
    projh = nc.dram_tensor("proj", [PM, DK], F32, kind="ExternalInput").ap()
    outh = nc.dram_tensor("out", [T, D], F32, kind="ExternalOutput").ap()
    scr = nc.dram_tensor("scratch", [8, 128], F32, kind="Internal").ap()

    def sb(name, shape, dt):
        return nc.alloc_sbuf_tensor(name, list(shape), dt).ap()

    def psum(name, shape):
        return nc.alloc_psum_tensor(name, list(shape), F32).ap()

    WT = {w: [sb(f"{w}T{k}", (128, D), BF16) for k in range(KC)]
          for w in ("wq", "wk", "wv", "wo", "wf")}
    qf = [sb(f"qf{h}", (128, SBN * CH), BF16) for h in range(H)]
    kf = [sb(f"kf{h}", (128, SBN * CH), BF16) for h in range(H)]
    kftm = [sb(f"kftm{h}", (128, SBN * CH), BF16) for h in range(H)]
    vext = [sb(f"vext{c}", (128, H * 65), BF16) for c in range(SBN)]
    stkA = sb("stkA", (128, SBN * CH), BF16)
    stkB = sb("stkB", (128, SBN * CH), BF16)
    projs = stkA[:, 0:DK]
    s_run = sb("s_run", (128, H * 65), F32)
    s_bf = sb("s_bf", (128, H * 65), BF16)
    am = [sb("am0", (128, 512), BF16), None]
    r4 = [sb(f"r4_{i}", (128, 8), F32) for i in range(3)]
    gtmp, gtmp2 = r4[0], r4[1]
    xs = [sb(f"xs{c}", (128, D), F32) for c in range(SBN)]
    x2 = [sb("x2_0", (128, D), F32)]
    xhat = [sb(f"xhat{c}", (128, D), BF16) for c in range(SBN)]
    fm = [sb(f"fm{k}", (128, SBN * CH), BF16) for k in range(KC)]
    atm = [sb(f"atm{c}", (128, D), BF16) for c in range(SBN)]
    h2b = sb("h2b", (128, D), BF16)
    qy = atm[0]
    qy2 = atm[1]
    qb1 = atm[2]
    qb2 = atm[3]
    qsi = h2b
    qst = sb("qst", (128, D), BF16)
    qn = sb("qn", (128, D), I16)
    qsc = sb("qsc", (128, 32), F32)
    qrs = sb("qrs", (128, 32), F32)
    qsb = sb("qsb", (128, 32), F32)
    xdt = xhat
    lst = sb("lst", (128, 4 * 12), F32)
    lmv = sb("lmv", (128, 4 * 2), F32)
    lt1 = sb("lt1", (128, 4), F32)
    lrstd = sb("lrstd", (128, 4), F32)
    lnegmr = sb("lnegmr", (128, 4), F32)
    c_lne = sb("c_lne", (128, 1), F32)
    c_negc0 = sb("c_negc0", (128, 1), F32)
    c_dene = sb("c_dene", (128, 1), F32)
    ident = sb("ident", (128, 128), BF16)
    idf32 = sb("idf32", (128, 128), F32)
    ones_row = sb("ones_row", (1, 128), BF16)
    projC = sb("projC", (128, 128), BF16)
    siluc = sb("siluc", (128, 8), F32)
    modT = sb("modT", (128, 48), F32)
    onca = sb("onca", (128, 8), F32)
    oncb = sb("oncb", (128, 8), F32)
    gacol = sb("gacol", (128, 8), F32)
    gbcol = sb("gbcol", (128, 8), F32)
    biasq = sb("biasq", (128, 8), F32)
    biask = sb("biask", (128, 8), F32)
    b8 = s_run[0:8, 0:128]
    c8 = b8
    mrow = b8
    row_f1 = None
    row_f2 = None
    row_bv = sb("row_bv", (1, D), BF16)
    row_gbo = sb("row_gbo", (1, D), BF16)
    row_gbf = sb("row_gbf", (1, D), BF16)
    row_bb = None
    oncb_b = sb("oncb_b", (128, D), BF16)
    sb_b = sb("sb_b", (128, D), BF16)
    wst = [xs[0], xs[1]]
    wsb = [xhat[0], xhat[1]]
    qf32 = [x2[0][:, 0:512], x2[0][:, 512:1024]]
    am[1] = x2[0].bitcast(BF16)[:, 1024:1536]
    row_f1 = xs[2][0:1, :]
    row_f2 = xs[3][0:1, :]
    row_bb = xhat[2][0:1, :]
    wmod_stage = [
        [qf[i].bitcast(F32) for i in range(16)] + [kf[i].bitcast(F32) for i in range(8)],
        [kf[8 + i].bitcast(F32) for i in range(8)] + [kftm[i].bitcast(F32) for i in range(16)],
    ]
    PA = psum("PA", (128, 1024))
    PB = psum("PB", (128, 1024))
    PT = psum("PT", (128, 512))
    PU = psum("PU", (128, 512))
    PO = psum("PO", (128, 512))
    PK = psum("PK", (128, 512))

    if verbose:
        print("sbuf bytes remaining:", nc.sbuf_bytes_remaining)

    bWT = {w: [Buf(f"{w}T{k}") for k in range(KC)] for w in WT}
    bqf = [Buf(f"qf{h}") for h in range(H)]
    bkf = [Buf(f"kf{h}") for h in range(H)]
    bkftm = [Buf(f"kftm{h}") for h in range(H)]
    bvext = [Buf(f"vext{c}") for c in range(SBN)]
    bstkA, bstkB = Buf("stkA"), Buf("stkB")
    bs_run = [Buf(f"s_run{g}") for g in range(4)]
    bs_bf = [Buf(f"s_bf{g}") for g in range(4)]
    bam = [Buf("am0"), None]
    br4 = [Buf(f"r4_{i}") for i in range(3)]
    bxs = [Buf(f"xs{c}") for c in range(SBN)]
    bx2 = [Buf("x2_0")]
    bxhat = [Buf(f"xhat{c}") for c in range(SBN)]
    bfm = [Buf(f"fm{k}") for k in range(KC)]
    batm = [Buf(f"atm{c}") for c in range(SBN)]
    bh2 = Buf("h2b")
    bqy, bqy2, bqb1, bqb2 = batm[0], batm[1], batm[2], batm[3]
    bqsi = bh2
    bqst, bqn = Buf("qst"), Buf("qn")
    bqsc, bqrs, bqsb = Buf("qsc"), Buf("qrs"), Buf("qsb")
    bxd = bxhat
    bln = [{n: Buf(n + str(i)) for n in ("lst", "lmv", "lt1", "lrstd", "lnegmr")}
           for i in range(4)]
    bconst = Buf("const")
    bmod = Buf("mod")
    bb8 = Buf("b8")
    bmrow = bb8
    brow1, brow2 = bxs[2], bxs[3]
    brow_bv, brow_gbo, brow_gbf = (Buf(x) for x in ("row_bv", "row_gbo", "row_gbf"))
    brow_bb = bxhat[2]
    boncb_b, bsb_b = Buf("oncb_b"), Buf("sb_b")
    bwst = [bxs[0], bxs[1]]
    bwsb = [bxhat[0], bxhat[1]]
    bqf32 = [bx2[0], bx2[0]]
    bam[1] = bx2[0]
    bwmod = [Buf("wmod0"), Buf("wmod1")]
    bwmod_alias = [[bqf[i] for i in range(16)] + [bkf[i] for i in range(8)],
                   [bkf[8 + i] for i in range(8)] + [bkftm[i] for i in range(16)]]
    bPA, bPB, bPT, bPU, bPO, bPK = (Buf(x) for x in ("PA", "PB", "PT", "PU", "PO", "PK"))
    bPAq, bPAk, bPBq, bPBk = (Buf(x) for x in ("PAq", "PAk", "PBq", "PBk"))
    bscr = Buf("scratch")
    bout = Buf("out")

    def fp6_round(tag, bsrc_y, sb_bb, bsb_scale):
        E("gpsimd", [lambda e: e.tensor_tensor(qy2, qy, qy, ALU.mult)],
          reads=[bsrc_y], writes=[bqy2])
        E("vector", [lambda e: e.tensor_scalar(qb1, qy2, 4.0, None, ALU.is_ge)],
          reads=[bqy2], writes=[bqb1])
        E("vector", [lambda e: e.tensor_scalar(qb2, qy2, 16.0, None, ALU.is_ge)],
          reads=[bqy2], writes=[bqb2])
        E("vector", [lambda e: e.tensor_scalar(qsi, qb1, -4.0, 8.0, ALU.mult, ALU.add)],
          reads=[bqb1], writes=[bqsi])
        E("vector", [lambda e: e.scalar_tensor_tensor(qsi, qb2, -2.0, qsi,
                                                      ALU.mult, ALU.add)],
          reads=[bqb2, bqsi], writes=[bqsi])
        E("gpsimd", [lambda e: e.tensor_scalar(qst, qb1, 0.125, 0.125,
                                               ALU.mult, ALU.add)],
          reads=[bqb1], writes=[bqst])
        E("vector", [lambda e: e.scalar_tensor_tensor(qst, qb2, 0.25, qst,
                                                      ALU.mult, ALU.add)],
          reads=[bqb2, bqst], writes=[bqst])
        E("vector", [lambda e: e.tensor_tensor(qn, qy, qsi, ALU.mult)],
          reads=[bsrc_y, bqsi], writes=[bqn])
        qsi3 = qsi.rearrange("p (b k) -> p b k", k=32)
        qst3 = qst.rearrange("p (b k) -> p b k", k=32)
        E("vector", [lambda e, qsi3=qsi3, qst3=qst3, sb_bb=sb_bb: e.tensor_tensor(
            qsi3, qst3, sb_bb, ALU.mult)],
          reads=[bqst, bqsb], writes=[bqsi])
        E("vector", [lambda e: e.tensor_tensor(qy, qn, qsi, ALU.mult)],
          reads=[bqn, bqsi], writes=[bqy])

    def quant_head(src_ap, bsrc, scale_col=None, bscale_extra=None):
        s3 = src_ap.rearrange("p (b k) -> p b k", k=32)
        E("vector", [lambda e, s3=s3: e.reduce_max(qsc.unsqueeze(2), s3, axis=AX.X,
                                                   apply_absolute_value=True)],
          reads=[bsrc], writes=[bqsc])
        E("vector", [lambda e: e.tensor_scalar(qsc, qsc, 1.0 / 7.5, 1e-30,
                                               ALU.mult, ALU.max)],
          reads=[bqsc], writes=[bqsc])
        E("vector", [lambda e: e.reciprocal(qrs, qsc)],
          reads=[bqsc], writes=[bqrs])
        if scale_col is not None:
            E("vector", [lambda e: e.tensor_scalar(qsb, qsc, scale_col, None,
                                                   ALU.mult)],
              reads=[bqsc, bscale_extra], writes=[bqsb])
        else:
            E("vector", [lambda e: e.tensor_copy(qsb, qsc)],
              reads=[bqsc], writes=[bqsb])
        rs_b = qrs.unsqueeze(2).broadcast_to((128, 32, 32))
        qy3 = qy.rearrange("p (b k) -> p b k", k=32)
        E("vector", [lambda e, s3=s3, qy3=qy3, rs_b=rs_b: e.tensor_tensor(
            qy3, s3, rs_b, ALU.mult)],
          reads=[bsrc, bqrs], writes=[bqy])
        sb_bb = qsb.unsqueeze(2).broadcast_to((128, 32, 32))
        fp6_round("q", bqy, sb_bb, bqsb)

    def stack_evac(pp_half, bpp, oc, bias_col, bbias, dA, dB, bdA, bdB):
        E("vector", [lambda e: e.tensor_scalar(
            dA[0:64, :], pp_half[0:64, :], bias_col[0:64, oc:oc + 1], None, ALU.add)],
          reads=[bpp, bbias], writes=[bdA])
        E("vector", [lambda e: e.tensor_scalar(
            dB[0:64, :], pp_half[64:128, :], bias_col[64:128, oc:oc + 1], None, ALU.add)],
          reads=[bpp, bbias], writes=[bdB])
        E("scalar", [lambda e: e.activation(
            dA[64:128, :], pp_half[0:64, :], AF.Square, bias=bias_col[0:64, oc:oc + 1])],
          reads=[bpp, bbias], writes=[bdA])
        E("scalar", [lambda e: e.activation(
            dB[64:128, :], pp_half[64:128, :], AF.Square,
            bias=bias_col[64:128, oc:oc + 1])],
          reads=[bpp, bbias], writes=[bdB])
        p.flush("scalar")

    flushbuf = lt1[:, 0:1]
    fdum = sb("fdum", (1, 8), F32)
    bfdum = Buf("fdum")

    fdum2 = sb("fdum2", (1, 8), F32)
    fdum3 = sb("fdum3", (1, 8), F32)
    p.flush_fns = {
        "vector": (lambda e: e.memset(fdum, 0.0), bfdum),
        "scalar": (lambda e: e.memzero(fdum2), Buf("fdum2")),
        "gpsimd": (lambda e: e.memset(fdum3, 0.0), Buf("fdum3")),
    }

    pu_rot = [(PU, bPU), (PK, bPK), (PO, bPO), (PT, bPT)]
    _urot = [0]

    def u_mm(stack, bstack, dst, bdst):
        pu, bpu = pu_rot[_urot[0] % 4]
        fscr, bfscr = qf32[_urot[0] % 2], bqf32[_urot[0] % 2]
        _urot[0] += 1
        E("tensor", [lambda e, pu=pu: e.matmul(pu, projC, stack, start=True, stop=True)],
          reads=[bconst, bstack], writes=[bpu])
        E("scalar", [lambda e, pu=pu, fscr=fscr: e.activation(
            fscr, pu, AF.Exp, bias=c_negc0)],
          reads=[bpu], writes=[bfscr])
        E("vector", [lambda e, fscr=fscr, dst=dst: e.tensor_scalar(
            dst, fscr, PHI_EPS, None, ALU.add)],
          reads=[bfscr], writes=[bdst])

    def fm_to_row(col_ap, breads, dst_row, bdst):
        E("tensor", [lambda e: e.matmul(PT[0:8, 0:128], col_ap, idf32,
                                        start=True, stop=True)],
          reads=breads + [bconst], writes=[bPT])
        E("vector", [lambda e: e.tensor_copy(mrow, PT[0:8, 0:128])],
          reads=[bPT], writes=[bmrow])
        E("scalar", [lambda e: e.dma_start(out=scr, in_=mrow)],
          reads=[bmrow], writes=[bscr], dma=True)
        E("scalar", [lambda e: e.dma_start(out=dst_row,
                                           in_=scr.rearrange("a b -> (a b)").unsqueeze(0))],
          reads=[bscr], writes=[bdst], dma=True)

    E("gpsimd", [lambda e: e.memset(ident, 1.0)], writes=[bconst])
    E("gpsimd", [lambda e: e.affine_select(ident, ident, [[-1, 128]], ALU.is_equal,
                                           0.0, base=0, channel_multiplier=1)],
      reads=[bconst], writes=[bconst])
    E("gpsimd", [lambda e: e.memset(idf32, 1.0)], writes=[bconst])
    E("gpsimd", [lambda e: e.affine_select(idf32, idf32, [[-1, 128]], ALU.is_equal,
                                           0.0, base=0, channel_multiplier=1)],
      reads=[bconst], writes=[bconst])
    E("vector", [lambda e: e.memset(ones_row, 1.0)], writes=[bconst])
    E("vector", [lambda e: e.memset(c_lne, LN_EPS)], writes=[bconst])
    E("vector", [lambda e: e.memset(c_negc0, -C0)], writes=[bconst])
    E("vector", [lambda e: e.memset(c_dene, DEN_EPS)], writes=[bconst])
    for c in range(SBN):
        vv = vext[c].rearrange("p (h n) -> p h n", n=65)[:, :, 64:65]
        E("vector", [lambda e, vv=vv: e.memset(vv, 1.0)], writes=[bvext[c]])

    E("sync", [lambda e: e.dma_start(out=c8, in_=ch.rearrange("(j p) -> j p", p=128))],
      writes=[bb8], dma=True)
    E("tensor", [lambda e: e.matmul(PT[:, 0:8], c8, idf32[0:8, 0:8],
                                    start=True, stop=True)],
      reads=[bb8, bconst], writes=[bPT])
    E("scalar", [lambda e: e.activation(siluc, PT[:, 0:8], AF.Silu)],
      reads=[bPT], writes=[bmod])

    E("scalar", [lambda e: e.dma_start(out=wst[0][:, 0:DK], in_=projh)],
      writes=[bwst[0]], dma=True)
    E("vector", [lambda e: e.tensor_scalar(projs, wst[0][:, 0:DK], S_XN, None,
                                           ALU.mult)],
      reads=[bwst[0]], writes=[bstkA])
    E("tensor", [lambda e: e.matmul(PT[0:DK, 0:128], projs, ident,
                                    start=True, stop=True)],
      reads=[bstkA, bconst], writes=[bPT])
    E("vector", [lambda e: e.tensor_copy(projC[0:DK, :], PT[0:DK, 0:128])],
      reads=[bPT], writes=[bconst])
    E("vector", [lambda e: e.memset(projC[DK:128, :], SQC)], writes=[bconst])

    for (hb, dst) in ((bh["bq"], biasq), (bh["bk"], biask)):
        E("scalar", [lambda e, hb=hb: e.dma_start(
            out=b8, in_=hb.rearrange("(j p) -> j p", p=128))],
          writes=[bb8], dma=True)
        E("tensor", [lambda e: e.matmul(PT[:, 8:16], b8, idf32[0:8, 0:8],
                                        start=True, stop=True)],
          reads=[bb8, bconst], writes=[bPT])
        E("vector", [lambda e, dst=dst: e.tensor_copy(dst, PT[:, 8:16])],
          reads=[bPT], writes=[bmod])

    E("scalar", [lambda e: e.dma_start(out=row_f1, in_=bh["bv"].unsqueeze(0))],
      writes=[brow1], dma=True)
    E("vector", [lambda e: e.tensor_copy(row_bv, row_f1)],
      reads=[brow1], writes=[brow_bv])

    def weight_prep(wname, scale_col=None, quantize=False):
        src = wh[wname].rearrange("(n p) k -> n p k", p=128)
        for oc in range(KC):
            st, bst = wst[oc % 2], bwst[oc % 2]
            cb, bcb = wsb[oc % 2], bwsb[oc % 2]
            q = "gpsimd" if oc % 2 else "scalar"
            E(q, [lambda e, oc=oc, st=st: e.dma_start(out=st, in_=src[oc])],
              writes=[bst], dma=True)
            if quantize:
                quant_head(st, bst, scale_col=scale_col[:, oc:oc + 1]
                           if scale_col is not None else None, bscale_extra=bmod)
                E("vector", [lambda e, cb=cb: e.tensor_copy(cb, qy)],
                  reads=[bqy], writes=[bcb])
            elif scale_col is not None:
                E("vector", [lambda e, st=st, cb=cb, oc=oc: e.tensor_scalar(
                    cb, st, scale_col[:, oc:oc + 1], None, ALU.mult)],
                  reads=[bst, bmod], writes=[bcb])
            else:
                E("vector", [lambda e, st=st, cb=cb: e.tensor_copy(cb, st)],
                  reads=[bst], writes=[bcb])
            for half in range(2):
                def mmz(e, cb=cb, half=half):
                    ins = None
                    for i in range(4):
                        kcc = half * 4 + i
                        ins = e.matmul(PT[:, i * 128:(i + 1) * 128],
                                       cb[:, kcc * 128:(kcc + 1) * 128],
                                       ident, start=True, stop=True)
                    return ins
                E("tensor", [mmz], reads=[bcb, bconst], writes=[bPT])

                def ev(e, wname=wname, oc=oc, half=half):
                    ins = None
                    for i in range(4):
                        kcc = half * 4 + i
                        ins = e.tensor_copy(
                            WT[wname][kcc][:, oc * 128:(oc + 1) * 128],
                            PT[:, i * 128:(i + 1) * 128])
                    return ins
                E("vector", [ev], reads=[bPT],
                  writes=[bWT[wname][half * 4 + i] for i in range(4)])

    weight_prep("wq")
    weight_prep("wk")
    weight_prep("wv")

    wm = wmodh.rearrange("(n p) j -> n p j", p=128)
    bwmodq = [[Buf(f"wmod{j}q{qi}") for qi in range(3)] for j in range(2)]
    for kcc in range(KC):
        j = kcc % 2
        stg = wmod_stage[j]
        for qi, q in enumerate(("sync", "scalar", "gpsimd")):
            idxs = [i for i in range(24) if i % 3 == qi]
            E(q, [(lambda e, kcc=kcc, i=i, stg=stg: e.dma_start(
                out=stg[i], in_=wm[kcc, :, i * 256:(i + 1) * 256]))
                for i in idxs],
              writes=[bwmodq[j][qi]] + [bwmod_alias[j][i] for i in idxs],
              dma=True)

        def modmm(e, kcc=kcc, stg=stg):
            ins = None
            for jt in range(48):
                seg, off = divmod(jt * 128, 256)
                ins = e.matmul(PU[:, jt:jt + 1], stg[seg][:, off:off + 128],
                               siluc[:, kcc:kcc + 1],
                               start=(kcc == 0 and jt == 0),
                               stop=(kcc == KC - 1 and jt == 47),
                               skip_group_check=True)
            return ins
        E("tensor", [modmm], reads=bwmodq[j] + [bmod], writes=[bPU])
    E("vector", [lambda e: e.tensor_copy(modT, PU[:, 0:48])],
      reads=[bPU], writes=[bmod])

    E("vector", [lambda e: e.tensor_scalar(onca, modT[:, 8:16], 1.0, None, ALU.add)],
      reads=[bmod], writes=[bmod])
    E("vector", [lambda e: e.tensor_scalar(oncb, modT[:, 32:40], 1.0, None, ALU.add)],
      reads=[bmod], writes=[bmod])
    for (sl0, sl1, dst) in ((16, 24, gacol), (40, 48, gbcol)):
        E("vector", [lambda e, sl0=sl0, sl1=sl1: e.tensor_scalar(
            gtmp, modT[:, sl0:sl1], 0.0, 2.0, ALU.is_ge, ALU.mult)],
          reads=[bmod], writes=[bmod, br4[0]])
        E("vector", [lambda e: e.tensor_scalar(gtmp, gtmp, -1.0, None, ALU.add)],
          reads=[bmod], writes=[bmod])
        E("scalar", [lambda e, sl0=sl0, sl1=sl1: e.activation(
            gtmp2, modT[:, sl0:sl1], AF.Abs)],
          reads=[bmod], writes=[bmod, br4[1]])
        E("vector", [lambda e: e.tensor_scalar(gtmp2, gtmp2, 0.25, None, ALU.max)],
          reads=[bmod], writes=[bmod])
        E("vector", [lambda e, dst=dst: e.tensor_tensor(dst, gtmp, gtmp2, ALU.mult)],
          reads=[bmod], writes=[bmod])

    for (colap, dstt, bd) in ((oncb, oncb_b, boncb_b), (modT[:, 24:32], sb_b, bsb_b)):
        fm_to_row(colap, [bmod], row_f1, brow1)
        E("vector", [lambda e: e.tensor_copy(row_bb, row_f1)],
          reads=[brow1], writes=[brow_bb])
        for half in range(2):
            E("tensor", [lambda e, half=half: e.matmul(
                PT[:, 0:512], ones_row, row_bb[:, half * 512:(half + 1) * 512],
                start=True, stop=True)],
              reads=[brow_bb, bconst], writes=[bPT])
            E("vector", [lambda e, half=half, dstt=dstt: e.tensor_copy(
                dstt[:, half * 512:(half + 1) * 512], PT[:, 0:512])],
              reads=[bPT], writes=[bd])

    for (gcol, bname, dst, bd) in ((gacol, "bo", row_gbo, brow_gbo),
                                   (gbcol, "bf", row_gbf, brow_gbf)):
        fm_to_row(gcol, [bmod], row_f1, brow1)
        E("scalar", [lambda e, bname=bname: e.dma_start(
            out=row_f2, in_=bh[bname].unsqueeze(0))],
          writes=[brow2], dma=True)
        E("vector", [lambda e, dst=dst: e.tensor_tensor(dst, row_f1, row_f2,
                                                        ALU.mult)],
          reads=[brow1, brow2], writes=[bd])

    E("gpsimd", [lambda e: e.memset(idf32, 1.0)], writes=[bconst])
    E("gpsimd", [lambda e: e.affine_select(idf32, idf32, [[1, 128]], ALU.is_ge,
                                           0.0, base=0, channel_multiplier=-1)],
      reads=[bconst], writes=[bconst])

    weight_prep("wo", scale_col=gacol)
    weight_prep("wf", scale_col=gbcol, quantize=True)

    E("vector", [lambda e: e.memset(s_run, 0.0)], reads=[bb8], writes=bs_run)
    E("vector", [lambda e: e.memset(s_bf, 0.0)], writes=bs_bf)

    pa_rot = [(PA, bPAq, bPAk), (PB, bPBq, bPBk)]
    pt_rot = [(PT, bPT), (PU, bPU)]
    out_tokens = []
    out3 = outh.rearrange("(n p) d -> n p d", p=128)

    def ln_chain(src_ap, bsrc, dst_ap, bdst, ci, feng="scalar"):
        L = bln[ci]
        _lst = lst[:, ci * 12:(ci + 1) * 12]
        _lmv = lmv[:, ci * 2:(ci + 1) * 2]
        _lt1 = lt1[:, ci:ci + 1]
        _lrstd = lrstd[:, ci:ci + 1]
        _lnegmr = lnegmr[:, ci:ci + 1]
        E("vector", [lambda e: e.bn_stats(_lst[:, 0:6], src_ap[:, 0:512]),
                     lambda e: e.bn_stats(_lst[:, 6:12], src_ap[:, 512:1024])],
          reads=[bsrc], writes=[L["lst"]])
        E("vector", [lambda e: e.bn_aggr(_lmv, _lst)],
          reads=[L["lst"]], writes=[L["lmv"]], strict=True)
        E("scalar", [lambda e: e.activation(_lt1, _lmv[:, 1:2], AF.Ln, bias=c_lne)],
          reads=[L["lmv"]], writes=[L["lt1"]])
        E("scalar", [lambda e: e.activation(_lrstd, _lt1, AF.Exp, scale=-0.5)],
          reads=[L["lt1"]], writes=[L["lrstd"]])
        E("vector", [lambda e: e.tensor_scalar(_lnegmr, _lmv[:, 0:1], _lrstd, -1.0,
                                               ALU.mult, ALU.mult)],
          reads=[L["lmv"], L["lrstd"]], writes=[L["lnegmr"]])
        if feng == "scalar":
            E("scalar", [lambda e: e.activation(dst_ap, src_ap, AF.Identity,
                                                bias=_lnegmr, scale=_lrstd)],
              reads=[bsrc, L["lrstd"], L["lnegmr"]], writes=[bdst])
        else:
            E("vector", [lambda e: e.tensor_scalar(dst_ap, src_ap, _lrstd,
                                                   _lnegmr, ALU.mult, ALU.add)],
              reads=[bsrc, L["lrstd"], L["lnegmr"]], writes=[bdst])
            p.flush("vector")

    st_sets = [
        (stkA, stkB, atm[0][:, 0:512], atm[0][:, 512:1024],
         bstkA, bstkB, batm[0], batm[0]),
        (atm[1][:, 0:512], atm[1][:, 512:1024],
         atm[2][:, 0:512], atm[2][:, 512:1024],
         batm[1], batm[1], batm[2], batm[2]),
    ]
    maskb = idf32.unsqueeze(1).broadcast_to((128, 4, 128))
    po_rot3 = [(PO, bPO), (PB[:, 512:1024], bPBk), (PA[:, 512:1024], bPAk)]
    pu_rot2 = [(PU, bPU), (PB[:, 0:512], bPBq), (PT, bPT)]
    pk_rot = [(PK, bPK), (PA[:, 0:512], bPAq)]

    def smalls_head(idx):
        c, g = divmod(idx, 4)
        hs = [4 * g + i for i in range(4)]
        POr, bPOr = po_rot3[idx % 3]
        PUr, bPUr = pu_rot2[idx % 3]
        ami, bami = am[idx % 2], bam[idx % 2]

        def inter_mm(e, hs=hs, c=c, POr=POr):
            ins = None
            for i, h in enumerate(hs):
                ins = e.matmul(POr[:, i * 65:(i + 1) * 65],
                               qf[h][:, c * 128:(c + 1) * 128],
                               s_bf[:, h * 65:(h + 1) * 65],
                               start=(i == 0), stop=False,
                               skip_group_check=True)
            return ins
        E("tensor", [inter_mm],
          reads=[bqf[h] for h in hs] + [bs_bf[g]], writes=[bPOr])

        def a_mm(e, hs=hs, c=c, PUr=PUr):
            ins = None
            for i, h in enumerate(hs):
                ins = e.matmul(PUr[:, i * 128:(i + 1) * 128],
                               kf[h][:, c * 128:(c + 1) * 128],
                               qf[h][:, c * 128:(c + 1) * 128],
                               start=(i == 0), stop=(i == 3))
            return ins
        E("tensor", [a_mm],
          reads=[bqf[h] for h in hs] + [bkf[h] for h in hs], writes=[bPUr])
        ami3 = ami.rearrange("p (h c) -> p h c", c=128)
        pu3 = PUr.rearrange("p (h c) -> p h c", c=128)
        E("vector", [lambda e, ami3=ami3, pu3=pu3: e.tensor_tensor(
            ami3, pu3, maskb, ALU.mult)],
          reads=[bPUr, bconst], writes=[bami])
        p.flush("vector")
        return (idx, c, g, hs, POr, bPOr, ami, bami)

    def smalls_tail(st):
        idx, c, g, hs, POr, bPOr, ami, bami = st
        PKr, bPKr = pk_rot[idx % 2]
        rr, brr = r4[idx % 3], br4[idx % 3]

        def intra_mm(e, hs=hs, c=c, ami=ami, POr=POr):
            ins = None
            for i, h in enumerate(hs):
                ins = e.matmul(POr[:, i * 65:(i + 1) * 65],
                               ami[:, i * 128:(i + 1) * 128],
                               vext[c][:, h * 65:(h + 1) * 65],
                               start=False, stop=(i == 3),
                               skip_group_check=True)
            return ins
        E("tensor", [intra_mm], reads=[bami, bvext[c]], writes=[bPOr])

        def kv_mm(e, hs=hs, c=c, PKr=PKr):
            ins = None
            for i, h in enumerate(hs):
                ins = e.matmul(PKr[:, i * 65:(i + 1) * 65],
                               kftm[h][:, c * 128:(c + 1) * 128],
                               vext[c][:, h * 65:(h + 1) * 65],
                               start=(i == 0), stop=(i == 3),
                               skip_group_check=True)
            return ins
        E("tensor", [kv_mm],
          reads=[bkftm[h] for h in hs] + [bvext[c]], writes=[bPKr])
        E("vector", [lambda e, g=g, PKr=PKr: e.tensor_tensor(
            s_run[:, g * 260:(g + 1) * 260],
            s_run[:, g * 260:(g + 1) * 260], PKr[:, 0:260], ALU.add)],
          reads=[bPKr, bs_run[g]], writes=[bs_run[g]])
        E("scalar", [lambda e, g=g: e.activation(
            s_bf[:, g * 260:(g + 1) * 260],
            s_run[:, g * 260:(g + 1) * 260], AF.Identity)],
          reads=[bs_run[g]], writes=[bs_bf[g]])
        p.flush("scalar")
        po3 = POr[:, 0:260].rearrange("p (h n) -> p h n", n=65)
        E("scalar", [lambda e, po3=po3, rr=rr: e.activation(
            rr[:, 4:8].unsqueeze(2), po3[:, :, 64:65], AF.Ln, bias=c_dene)],
          reads=[bPOr, bconst], writes=[brr])
        E("scalar", [lambda e, rr=rr: e.activation(
            rr[:, 0:4], rr[:, 4:8], AF.Exp, scale=-1.0)],
          reads=[brr], writes=[brr])
        return (c, g, po3, bPOr, rr, brr)

    def smalls_at3(t2):
        c, g, po3, bPOr, rr, brr = t2
        at3 = atm[c].rearrange("p (h d) -> p h d", d=64)[:, 4 * g:4 * g + 4, :]
        E("vector", [lambda e, po3=po3, rr=rr, at3=at3: e.tensor_tensor(
            at3, po3[:, :, 0:64],
            rr[:, 0:4].unsqueeze(2).broadcast_to((128, 4, 64)), ALU.mult)],
          reads=[bPOr, brr], writes=[batm[c]])

    for sbi in range(NSB):
        for c in range(SBN):
            gc = sbi * SBN + c
            E("sync", [lambda e, gc=gc, c=c: e.dma_start(
                out=xs[c], in_=xh.rearrange("(n p) d -> n p d", p=128)[gc])],
              writes=[bxs[c]], dma=True)
            ln_chain(xs[c], bxs[c], xhat[c], bxhat[c], c, feng="vector")
            for half in range(2):
                PTr, bPTr = pt_rot[half]

                def tmm(e, c=c, half=half, PTr=PTr):
                    ins = None
                    for i in range(4):
                        kcc = half * 4 + i
                        ins = e.matmul(PTr[:, i * 128:(i + 1) * 128],
                                       xhat[c][:, kcc * 128:(kcc + 1) * 128],
                                       ident, start=True, stop=True)
                    return ins
                E("tensor", [tmm], reads=[bxhat[c], bconst], writes=[bPTr])

                def fold(e, c=c, half=half, PTr=PTr):
                    ins = None
                    for i in range(4):
                        kcc = half * 4 + i
                        ins = e.tensor_scalar(
                            fm[kcc][:, c * 128:(c + 1) * 128],
                            PTr[:, i * 128:(i + 1) * 128],
                            onca[:, kcc:kcc + 1], modT[:, kcc:kcc + 1],
                            ALU.mult, ALU.add)
                    return ins
                E("vector", [fold], reads=[bPTr, bmod],
                  writes=[bfm[half * 4 + i] for i in range(4)])
            p.flush("vector")

        def u_mm4(po):
            sA, sB, sC, sD, bA, bB, bC, bD = st_sets[po % 2]
            u_mm(sA, bA, qf[po * 2], bqf[po * 2])
            u_mm(sB, bB, qf[po * 2 + 1], bqf[po * 2 + 1])
            u_mm(sC, bC, kf[po * 2], bkf[po * 2])
            u_mm(sD, bD, kf[po * 2 + 1], bkf[po * 2 + 1])

        pend_u = None
        for oc in range(KC):
            pp, bppq, bppk = pa_rot[oc % 2]
            sA, sB, sC, sD, bA, bB, bC, bD = st_sets[oc % 2]

            def qmm(e, oc=oc, pp=pp):
                ins = None
                for kcc in range(KC):
                    ins = e.matmul(pp[:, 0:512],
                                   WT["wq"][kcc][:, oc * 128:(oc + 1) * 128],
                                   fm[kcc], start=(kcc == 0), stop=(kcc == KC - 1))
                return ins
            E("tensor", [qmm], reads=bfm + bWT["wq"], writes=[bppq])

            def kmm(e, oc=oc, pp=pp):
                ins = None
                for kcc in range(KC):
                    ins = e.matmul(pp[:, 512:1024],
                                   WT["wk"][kcc][:, oc * 128:(oc + 1) * 128],
                                   fm[kcc], start=(kcc == 0), stop=(kcc == KC - 1))
                return ins
            E("tensor", [kmm], reads=bfm + bWT["wk"], writes=[bppk])
            if pend_u is not None:
                u_mm4(pend_u)
            stack_evac(pp[:, 0:512], bppq, oc, biasq, bmod, sA, sB, bA, bB)
            stack_evac(pp[:, 512:1024], bppk, oc, biask, bmod, sC, sD, bC, bD)
            pend_u = oc
        u_mm4(pend_u)
        p.flush("vector")
        p.flush("gpsimd")

        for c in range(SBN):
            pp, bppq, bppk = pa_rot[c % 2]

            def vmm(e, c=c, pp=pp):
                ins = None
                for half in range(2):
                    for kcc in range(KC):
                        ins = e.matmul(pp[:, half * 512:(half + 1) * 512],
                                       fm[kcc][:, c * 128:(c + 1) * 128],
                                       WT["wv"][kcc][:, half * 512:(half + 1) * 512],
                                       start=(kcc == 0), stop=False)
                    ins = e.matmul(pp[:, half * 512:(half + 1) * 512], ones_row,
                                   row_bv[:, half * 512:(half + 1) * 512],
                                   start=False, stop=True)
                return ins
            E("tensor", [vmm], reads=bfm + bWT["wv"] + [brow_bv, bconst],
              writes=[bppq, bppk])
            vv = vext[c].rearrange("p (h n) -> p h n", n=65)[:, :, 0:64]
            E("vector", [lambda e, pp=pp, vv=vv: e.tensor_copy(
                vv, pp.rearrange("p (h d) -> p h d", d=64))],
              reads=[bppq, bppk], writes=[bvext[c]])
            p.flush("vector")

        for h in range(H):
            PTr, bPTr = pt_rot[h % 2]

            def tmm(e, h=h, PTr=PTr):
                ins = None
                for c in range(SBN):
                    ins = e.matmul(PTr[:, c * 128:(c + 1) * 128],
                                   kf[h][:, c * 128:(c + 1) * 128], ident,
                                   start=True, stop=True)
                return ins
            E("tensor", [tmm], reads=[bkf[h], bconst], writes=[bPTr])
            E("vector", [lambda e, h=h, PTr=PTr: e.tensor_copy(kftm[h], PTr)],
              reads=[bPTr], writes=[bkftm[h]])
        p.flush("vector")

        pend = None
        pend2 = None
        for c in range(SBN):
            for g in range(4):
                st = smalls_head(c * 4 + g)
                if pend is not None:
                    t2 = smalls_tail(pend)
                    if pend2 is not None:
                        smalls_at3(pend2)
                    pend2 = t2
                pend = st
        t2 = smalls_tail(pend)
        smalls_at3(pend2)
        smalls_at3(t2)

        for kcc in range(KC):
            PTr, bPTr = pt_rot[kcc % 2]

            def tmm(e, kcc=kcc, PTr=PTr):
                ins = None
                for c in range(SBN):
                    ins = e.matmul(PTr[:, c * 128:(c + 1) * 128],
                                   atm[c][:, kcc * 128:(kcc + 1) * 128], ident,
                                   start=True, stop=True)
                return ins
            E("tensor", [tmm], reads=batm + [bconst], writes=[bPTr])
            if kcc % 2 == 0:
                E("vector", [lambda e, kcc=kcc, PTr=PTr: e.tensor_copy(
                    fm[kcc], PTr)],
                  reads=[bPTr], writes=[bfm[kcc]])
            else:
                E("scalar", [lambda e, kcc=kcc, PTr=PTr: e.activation(
                    fm[kcc], PTr, AF.Identity)],
                  reads=[bPTr], writes=[bfm[kcc]])
        p.flush("vector")
        p.flush("scalar")

        for c in range(SBN):
            gc = sbi * SBN + c
            pp, bppq, bppk = pa_rot[c % 2]

            def womm(e, c=c, pp=pp):
                ins = None
                for half in range(2):
                    for kcc in range(KC):
                        ins = e.matmul(pp[:, half * 512:(half + 1) * 512],
                                       fm[kcc][:, c * 128:(c + 1) * 128],
                                       WT["wo"][kcc][:, half * 512:(half + 1) * 512],
                                       start=(kcc == 0), stop=False)
                    ins = e.matmul(pp[:, half * 512:(half + 1) * 512], ones_row,
                                   row_gbo[:, half * 512:(half + 1) * 512],
                                   start=False, stop=True)
                return ins
            E("tensor", [womm], reads=bfm + bWT["wo"] + [brow_gbo, bconst],
              writes=[bppq, bppk])
            E("vector", [lambda e, c=c, pp=pp: e.tensor_tensor(
                xs[c], xs[c], pp, ALU.add)],
              reads=[bppq, bppk, bxs[c]], writes=[bxs[c]])
            ln_chain(xs[c], bxs[c], h2b, bh2, c)
            E("vector", [lambda e: e.tensor_tensor(h2b, h2b, oncb_b, ALU.mult)],
              reads=[bh2, boncb_b], writes=[bh2])
            E("vector", [lambda e: e.tensor_tensor(h2b, h2b, sb_b, ALU.add)],
              reads=[bh2, bsb_b], writes=[bh2])
            quant_head(h2b, bh2)
            E("vector", [lambda e, c=c: e.tensor_copy(xdt[c], qy)],
              reads=[bqy], writes=[bxd[c]])
            p.flush("vector")

        for c in range(SBN):
            for half in range(2):
                PTr, bPTr = pt_rot[half]

                def tmm(e, c=c, half=half, PTr=PTr):
                    ins = None
                    for i in range(4):
                        kcc = half * 4 + i
                        ins = e.matmul(PTr[:, i * 128:(i + 1) * 128],
                                       xdt[c][:, kcc * 128:(kcc + 1) * 128],
                                       ident, start=True, stop=True)
                    return ins
                E("tensor", [tmm], reads=[bxd[c], bconst], writes=[bPTr])
                if half == 0:
                    def fmc(e, c=c, half=half, PTr=PTr):
                        ins = None
                        for i in range(4):
                            kcc = half * 4 + i
                            ins = e.tensor_copy(
                                fm[kcc][:, c * 128:(c + 1) * 128],
                                PTr[:, i * 128:(i + 1) * 128])
                        return ins
                    E("vector", [fmc], reads=[bPTr],
                      writes=[bfm[half * 4 + i] for i in range(4)])
                else:
                    def fmc(e, c=c, half=half, PTr=PTr):
                        ins = None
                        for i in range(4):
                            kcc = half * 4 + i
                            ins = e.activation(
                                fm[kcc][:, c * 128:(c + 1) * 128],
                                PTr[:, i * 128:(i + 1) * 128], AF.Identity)
                        return ins
                    E("scalar", [fmc], reads=[bPTr],
                      writes=[bfm[half * 4 + i] for i in range(4)])
        p.flush("vector")
        p.flush("scalar")
        for c in range(SBN):
            gc = sbi * SBN + c
            pp, bppq, bppk = pa_rot[c % 2]

            def ffmm(e, c=c, pp=pp):
                ins = None
                for half in range(2):
                    for kcc in range(KC):
                        ins = e.matmul(pp[:, half * 512:(half + 1) * 512],
                                       fm[kcc][:, c * 128:(c + 1) * 128],
                                       WT["wf"][kcc][:, half * 512:(half + 1) * 512],
                                       start=(kcc == 0), stop=False)
                    ins = e.matmul(pp[:, half * 512:(half + 1) * 512], ones_row,
                                   row_gbf[:, half * 512:(half + 1) * 512],
                                   start=False, stop=True)
                return ins
            E("tensor", [ffmm], reads=bfm + bWT["wf"] + [brow_gbf, bconst],
              writes=[bppq, bppk])
            E("vector", [lambda e, c=c, pp=pp: e.tensor_tensor(
                xs[c], xs[c], pp, ALU.add)],
              reads=[bppq, bppk, bxs[c]], writes=[bxs[c]])
            tok = E("sync", [lambda e, gc=gc, c=c: e.dma_start(
                out=out3[gc], in_=xs[c])],
                reads=[bxs[c]], writes=[bout], dma=True)
            out_tokens.append(tok)

    p.finalize(out_tokens[-1:] if out_tokens else [])
    if verbose:
        print("groups:", p.n_groups)
    return nc


def kernel(L0_attn_x, L0_attn_Wq, L0_attn_bq, L0_attn_Wk, L0_attn_bk,
           L0_attn_Wv, L0_attn_bv, L0_attn_Wo, L0_attn_bo, L0_attn_proj,
           L0_ffn_weight, L0_ffn_bias, L0_c, L0_wmod):
    x = np.ascontiguousarray(np.asarray(L0_attn_x, dtype=np.float32))
    nc = build_core_program()
    common = {
        "wq": np.ascontiguousarray(L0_attn_Wq, dtype=np.float32),
        "wk": np.ascontiguousarray(L0_attn_Wk, dtype=np.float32),
        "wv": np.ascontiguousarray(L0_attn_Wv, dtype=np.float32),
        "wo": np.ascontiguousarray(L0_attn_Wo, dtype=np.float32),
        "wf": np.ascontiguousarray(L0_ffn_weight, dtype=np.float32),
        "bq": np.ascontiguousarray(L0_attn_bq, dtype=np.float32),
        "bk": np.ascontiguousarray(L0_attn_bk, dtype=np.float32),
        "bv": np.ascontiguousarray(L0_attn_bv, dtype=np.float32),
        "bo": np.ascontiguousarray(L0_attn_bo, dtype=np.float32),
        "bf": np.ascontiguousarray(L0_ffn_bias, dtype=np.float32),
        "proj": np.ascontiguousarray(L0_attn_proj, dtype=np.float32),
        "wmod": np.ascontiguousarray(L0_wmod, dtype=np.float32),
    }
    in_maps = []
    for b in range(8):
        m = dict(common)
        m["x"] = np.ascontiguousarray(x[b])
        m["c"] = np.ascontiguousarray(np.asarray(L0_c, dtype=np.float32)[b])
        in_maps.append(m)
    res = run_bass_kernel_spmd(nc, in_maps, core_ids=list(range(8)))
    return np.stack([res.results[b]["out"] for b in range(8)], axis=0)
